# version 01
# baseline (speedup 1.0000x reference)
import math
import numpy as np
import concourse.bass as bass
import concourse.mybir as mybir
from concourse.bass_utils import run_bass_kernel_spmd

F32 = mybir.dt.float32
BF16 = mybir.dt.bfloat16
AF = mybir.ActivationFunctionType
ALU = mybir.AluOpType

L = 2048
D = 1024
E = 2048
TWO_PI = 2.0 * math.pi
INV_2PI = 1.0 / TWO_PI
RND_MAGIC = 12582912.0
RMS_EPS_A = float(np.finfo(np.float32).eps)
LN_EPS = 1e-5



class MockInst:
    def then_inc(self, sem, n):
        sem.n += n
        return self


class Inst:
    def __init__(self, real):
        self.real = real

    def then_inc(self, sem, n):
        sem.n += n
        self.real.then_inc(sem.h, n)
        return self


class Sem:
    def __init__(self, h):
        self.h = h
        self.n = 0


class Eng:
    def __init__(self, real):
        self.real = real

    def wait(self, sem, val):
        if self.real is not None:
            self.real.wait_ge(sem.h, val)

    def op(self, name, *a, **k):
        if self.real is None:
            return MockInst()
        return Inst(getattr(self.real, name)(*a, **k))

    def dma(self, out, in_, **k):
        return self.op("dma_start", out=out, in_=in_, **k)

    def tt(self, out, a, b, op=ALU.mult):
        return self.op("tensor_tensor", out=out, in0=a, in1=b, op=op)

    def ts(self, out, a, s1, s2=None, op0=ALU.mult, op1=ALU.bypass):
        return self.op("tensor_scalar", out=out, in0=a, scalar1=s1, scalar2=s2, op0=op0, op1=op1)

    def stt(self, out, in0, scalar, in1, op0=ALU.mult, op1=ALU.add):
        return self.op("scalar_tensor_tensor", out=out, in0=in0, scalar=scalar, in1=in1, op0=op0, op1=op1)

    def act(self, out, in_, func, **k):
        return self.op("activation", out=out, in_=in_, func=func, **k)

    def copy(self, out, in_):
        return self.op("tensor_copy", out=out, in_=in_)

    def mm(self, out, lhsT, rhs, start, stop):
        return self.op("matmul", out=out, lhsT=lhsT, rhs=rhs, start=start, stop=stop)

    def scan(self, out, a, b, initial=0.0):
        return self.op("tensor_tensor_scan", out=out, data0=a, data1=b, initial=initial,
                       op0=ALU.mult, op1=ALU.add)

    def memset(self, ap, c):
        return self.op("memset", ap, c)

    def inc(self, sem, n=1):
        sem.n += n
        if self.real is not None:
            self.real.sem_inc(sem.h, n)


class B:
    def __init__(self, engines, sems):
        self.t = Eng(engines.get("tensor"))
        self.v = Eng(engines.get("vector"))
        self.s = Eng(engines.get("scalar"))
        self.y = Eng(engines.get("sync"))
        self.g = Eng(engines.get("gpsimd"))
        (self.PE, self.V, self.S, self.Y, self.G, self.YP, self.GP,
         self.GS0, self.GS1, self.GS2, self.YS0, self.YS1) = sems
        self.GS = [self.GS0, self.GS1, self.GS2]
        self.YS = [self.YS0, self.YS1]
        self.all_engines = [self.t, self.v, self.s, self.y]
        self.all_sems = [self.PE, self.V, self.S, self.YP]

    def g_sync(self):
        for sem in (self.PE, self.V, self.S, self.YP):
            self.g.wait(sem, sem.n)

    def barrier(self):
        self.y.wait(self.Y, self.Y.n)
        self.y.wait(self.YS0, self.YS0.n)
        self.y.wait(self.YS1, self.YS1.n)
        for eng, sem in zip(self.all_engines, self.all_sems):
            sem.n += 1
            if eng.real is not None:
                eng.real.drain().then_inc(sem.h, 1)
        vals = [sm.n for sm in self.all_sems]
        for eng in self.all_engines:
            for sem, val in zip(self.all_sems, vals):
                eng.wait(sem, val)


def ceil_div(a, b):
    return (a + b - 1) // b


class Arena:
    def __init__(self, handle, n_el):
        self.h = handle
        self.n = n_el
        self.live = {}

    def alloc(self, name, n_bf16, align=32):
        assert name not in self.live, name
        n_bf16 = ceil_div(n_bf16, align) * align
        taken = sorted(self.live.values())
        off = 0
        for o, sz in taken:
            if off + n_bf16 <= o:
                break
            off = max(off, o + sz)
        assert off + n_bf16 <= self.n, \
            f"arena overflow {name}: need {n_bf16} at {off}/{self.n} live={sorted(self.live.items(), key=lambda kv: kv[1])}"
        self.live[name] = (off, n_bf16)
        return off

    def free(self, *names):
        for name in names:
            del self.live[name]

    def bf(self, name):
        off, size = self.live[name]
        return self.h.ap()[:, off:off + size]

    def f32(self, name):
        off, size = self.live[name]
        return self.h.ap()[:, off:off + size].bitcast(F32)



def prep_inputs(inputs, b):
    ct = np.ascontiguousarray
    f = np.float32
    d = {}
    d["xr0"] = ct(np.asarray(inputs["L0_attn_a_x"][b]).T).astype(f)
    d["a_WinT"] = ct(np.asarray(inputs["L0_attn_a_W_in"]).T)
    W_x = np.asarray(inputs["L0_attn_a_W_x"])
    ev = list(range(0, 16, 2))
    od = list(range(1, 16, 2))
    d["a_WxBe"] = ct(W_x[0:16][ev].T)
    d["a_WxBo"] = ct(W_x[0:16][od].T)
    d["a_WxCe"] = ct(W_x[16:32][ev].T)
    d["a_WxCo"] = ct(W_x[16:32][od].T)
    d["a_wdt_col"] = ct(W_x[32].reshape(16, 128).T)
    d["a_wlam_col"] = ct(W_x[33].reshape(16, 128).T)
    d["a_Wdt_row"] = ct(np.asarray(inputs["L0_attn_a_W_dt"]).reshape(1, 2048))
    d["a_bdt"] = ct(np.asarray(inputs["L0_attn_a_b_dt"]).reshape(16, 128).T)
    d["a_Alog"] = ct(np.asarray(inputs["L0_attn_a_A_log"]).reshape(16, 128, 16).transpose(1, 0, 2).reshape(128, 256))

    bb_ = np.asarray(inputs["L0_attn_a_B_bias"])
    cb_ = np.asarray(inputs["L0_attn_a_C_bias"])
    bw_ = np.asarray(inputs["L0_attn_a_B_norm_w"])
    cw_ = np.asarray(inputs["L0_attn_a_C_norm_w"])
    d["a_biasBe"] = ct(bb_[ev].reshape(8, 1)); d["a_biasBo"] = ct(bb_[od].reshape(8, 1))
    d["a_biasCe"] = ct(cb_[ev].reshape(8, 1)); d["a_biasCo"] = ct(cb_[od].reshape(8, 1))
    d["a_normBe"] = ct(bw_[ev].reshape(8, 1)); d["a_normBo"] = ct(bw_[od].reshape(8, 1))
    d["a_normCe"] = ct(cw_[ev].reshape(8, 1)); d["a_normCo"] = ct(cw_[od].reshape(8, 1))
    d["a_freq_row"] = ct(np.asarray(inputs["L0_attn_a_rope_freqs"]).reshape(1, 8))
    d["a_Da"] = ct(np.asarray(inputs["L0_attn_a_D"]).reshape(16, 128).T)
    d["a_WoutT"] = ct(np.asarray(inputs["L0_attn_a_W_out"]).T)
    d["f_w1T"] = ct(np.asarray(inputs["L0_ffn_w1_w"]).T)
    d["f_w2T"] = ct(np.asarray(inputs["L0_ffn_w2_w"]).T)
    d["f_h1T"] = ct(np.asarray(inputs["L0_ffn_h1_w"]).T)
    d["f_h2T"] = ct(np.asarray(inputs["L0_ffn_h2_w"]).T)
    d["f_w3T"] = ct(np.asarray(inputs["L0_ffn_w3_w"]).T)
    d["f_s1"] = ct(np.asarray(inputs["L0_ffn_w1_s"]).reshape(32, 128).T)
    d["f_s2"] = ct(np.asarray(inputs["L0_ffn_w2_s"]).reshape(32, 128).T)
    d["f_s3"] = ct(np.asarray(inputs["L0_ffn_h1_s"]).reshape(32, 128).T)
    d["f_s4"] = ct(np.asarray(inputs["L0_ffn_h2_s"]).reshape(32, 128).T)
    d["f_s5"] = ct(np.asarray(inputs["L0_ffn_w3_s"]).reshape(8, 128).T)
    d["b_inT"] = ct(np.asarray(inputs["L0_attn_b_in_proj_w"]).T)
    d["b_cw"] = ct(np.asarray(inputs["L0_attn_b_conv_w"]).reshape(17, 128, 4).transpose(1, 0, 2).reshape(128, 68))
    d["b_cb"] = ct(np.asarray(inputs["L0_attn_b_conv_b"]).reshape(17, 128).T)
    d["b_dtb"] = ct(np.asarray(inputs["L0_attn_b_dt_bias"]).reshape(32, 1))
    d["b_Alog"] = ct(np.asarray(inputs["L0_attn_b_A_log"]).reshape(32, 1))
    d["b_D_row"] = ct(np.asarray(inputs["L0_attn_b_D"]).reshape(1, 32))
    d["b_nw_row"] = ct(np.asarray(inputs["L0_attn_b_ssm_norm_w"]).reshape(1, 2048))
    d["b_rmsw"] = ct(np.asarray(inputs["L0_attn_b_rms_w"]).reshape(8, 128).T)
    d["b_outT"] = ct(np.asarray(inputs["L0_attn_b_out_proj_w"]).T)
    d["b_fc1T"] = ct(np.asarray(inputs["L0_attn_b_fc1_w"]).T)
    d["b_fc2T"] = ct(np.asarray(inputs["L0_attn_b_fc2_w"]).T)
    return d


CONST_SPECS = [("ident", [128, 128], BF16), ("iot", [128, 128], F32),
               ("ones_f", [128, 1], F32), ("ones_b", [128, 1], BF16),
               ("cAlog", [128, 256], F32), ("A", [128, 256], F32),
               ("bdt", [128, 16], F32), ("Da", [128, 16], F32),
               ("biasBe", [8, 1], F32), ("biasBo", [8, 1], F32),
               ("biasCe", [8, 1], F32), ("biasCo", [8, 1], F32),
               ("normBe", [8, 1], F32), ("normBo", [8, 1], F32),
               ("normCe", [8, 1], F32), ("normCo", [8, 1], F32),
               ("mini", [1, 64], F32)]


def declare(nc, taps=()):
    T = {}

    def ei(name, shape, dt=F32):
        T[name] = nc.dram_tensor(name, shape, dt, kind="ExternalInput").ap()

    ei("xr0", [D, L])
    ei("a_WinT", [D, 2 * E])
    for q in ("Be", "Bo", "Ce", "Co"):
        ei("a_Wx" + q, [E, 8])
    ei("a_wdt_col", [128, 16])
    ei("a_wlam_col", [128, 16])
    ei("a_Wdt_row", [1, E])
    ei("a_bdt", [128, 16])
    ei("a_Alog", [128, 256])
    for q in ("Be", "Bo", "Ce", "Co"):
        ei("a_bias" + q, [8, 1])
        ei("a_norm" + q, [8, 1])
    ei("a_freq_row", [1, 8])
    ei("a_Da", [128, 16])
    ei("a_WoutT", [E, D])
    ei("f_w1T", [D, 4096]); ei("f_w2T", [D, 4096])
    ei("f_h1T", [4096, 4096]); ei("f_h2T", [4096, 4096])
    ei("f_w3T", [4096, D])
    ei("f_s1", [128, 32]); ei("f_s2", [128, 32])
    ei("f_s3", [128, 32]); ei("f_s4", [128, 32]); ei("f_s5", [128, 8])
    ei("b_inT", [D, 4256]); ei("b_cw", [128, 68]); ei("b_cb", [128, 17])
    ei("b_dtb", [32, 1]); ei("b_Alog", [32, 1]); ei("b_D_row", [1, 32])
    ei("b_nw_row", [1, 2048]); ei("b_rmsw", [128, 8])
    ei("b_outT", [E, D]); ei("b_fc1T", [D, 8192]); ei("b_fc2T", [4096, D])
    dbg_kind = "Internal"
    for nm, shape, dt, kd in [("sp_zs", [E, L], BF16, "Internal"), ("sp_dt", [E, L], BF16, "Internal"),
                          ("sp_kxp", [E, L], BF16, "Internal"), ("sp_T2", [E, L], BF16, "Internal"),
                          ("sp_yz", [E, L], BF16, "Internal"), ("xr1", [D, L], F32, "Internal"),
                          ("sp_rows", [8, L], F32, "Internal"), ("sp_br", [32, L], BF16, "Internal"),
                          ("dbg1", [32, L], F32, "Internal"), ("dbg2", [128, 16 * 1026], F32, dbg_kind),
                          ("sp_h1", [4096, L], BF16, "Internal"), ("sp_h2", [4096, L], BF16, "Internal"),
                          ("xr2", [D, L], F32, "Internal"), ("xr3", [D, L], F32, "Internal"),
                          ("sp_es", [32, L], F32, "Internal"), ("sp_esn", [32, L], F32, "Internal"),
                          ("sp_xb", [D, L], F32, "Internal"),
                          ("dbg3", [128, 16 * 1026], BF16, dbg_kind), ("dbg4", [128, 16 * 1026], BF16, dbg_kind)]:
        T[nm] = nc.dram_tensor(nm, shape, dt, kind=kd).ap()
    T["out"] = nc.dram_tensor("out", [D, L], F32, kind="ExternalOutput").ap()
    TAPS = {}
    for nm, shape in taps:
        TAPS[nm] = nc.dram_tensor("tap_" + nm, list(shape), F32, kind="ExternalOutput").ap()
    return T, TAPS



def ln_feature_major(bx, ar, PSA, PSB, C, T, x_dram, xn_out, n_dtiles=8):
    t, v, s, y, g = bx.t, bx.v, bx.s, bx.y, bx.g
    PE, V, S, Y, G = bx.PE, bx.V, bx.S, bx.Y, bx.G
    inv_d = 1.0 / (128 * n_dtiles)
    for nm, sz in (("xslot0", 4096), ("xslot1", 4096), ("lnrows", 4 * L * 2),
                   ("lntmp", L * 2), ("lntmp2", L * 2), ("bcast1", L * 2), ("bcast2", L * 2)):
        ar.alloc(nm, sz)
    RO = ar.f32("lnrows")

    def row(i):
        return RO[0:1, i * L:(i + 1) * L]
    TMP = [ar.f32("lntmp"), ar.f32("lntmp2")]
    XS = [ar.f32("xslot0"), ar.f32("xslot1")]
    vmark = [0] * n_dtiles
    smark = [0] * n_dtiles
    gmark = [0] * n_dtiles
    bx.g_sync()
    ins = None
    for kt in range(n_dtiles):
        gs = bx.GS[kt % 2]
        if kt >= 2:
            g.wait(PE, vmark[kt - 2])
        g.dma(out=XS[kt % 2], in_=x_dram[kt * 128:(kt + 1) * 128, :]).then_inc(gs, 16)
        gmark[kt] = gs.n
        s.wait(gs, gmark[kt])
        if kt >= 2:
            s.wait(PE, vmark[kt - 2])
        s.act(TMP[kt % 2], XS[kt % 2], AF.Square).then_inc(S, 1)
        smark[kt] = S.n
        t.wait(S, smark[kt])
        for c in range(4):
            t.mm(PSA.ap()[0:1, c * 512:(c + 1) * 512], C["ones_f"][:],
                 XS[kt % 2][:, c * 512:(c + 1) * 512], start=(kt == 0), stop=(kt == n_dtiles - 1))
            ins = t.mm(PSB.ap()[0:1, c * 512:(c + 1) * 512], C["ones_f"][:],
                       TMP[kt % 2][:, c * 512:(c + 1) * 512], start=(kt == 0), stop=(kt == n_dtiles - 1))
        ins.then_inc(PE, 1)
        vmark[kt] = PE.n
    v.wait(PE, PE.n)
    v.ts(row(0), PSA.ap()[0:1, 0:L], inv_d)
    v.ts(row(1), PSB.ap()[0:1, 0:L], inv_d)
    v.tt(row(2), row(0), row(0), op=ALU.mult)
    v.tt(row(1), row(1), row(2), op=ALU.subtract).then_inc(V, 1)
    s.wait(V, V.n)
    s.act(row(2), row(1), AF.Ln, bias=LN_EPS)
    s.act(row(2), row(2), AF.Exp, scale=-0.5).then_inc(S, 1)
    v.wait(S, S.n)
    v.tt(row(3), row(0), row(2), op=ALU.mult).then_inc(V, 1)
    y.wait(V, V.n)
    y.dma(out=T["sp_rows"][0:1, :], in_=row(2)).then_inc(Y, 16)
    y.dma(out=T["sp_rows"][1:2, :], in_=row(3)).then_inc(Y, 16)
    y.wait(Y, Y.n)
    y.dma(out=ar.f32("bcast1"), in_=T["sp_rows"][0:1, :].partition_broadcast(128)).then_inc(Y, 16)
    y.dma(out=ar.f32("bcast2"), in_=T["sp_rows"][1:2, :].partition_broadcast(128)).then_inc(Y, 16)
    ymark = Y.n
    for kt in range(n_dtiles):
        gs = bx.GS[kt % 2]
        if kt >= 2:
            g.wait(V, V.n)
        else:
            g.wait(PE, vmark[n_dtiles - 1])
        g.dma(out=XS[kt % 2], in_=x_dram[kt * 128:(kt + 1) * 128, :]).then_inc(gs, 16)
        v.wait(gs, gs.n)
        v.wait(Y, ymark)
        v.tt(TMP[kt % 2], XS[kt % 2], ar.f32("bcast1"), op=ALU.mult)
        v.tt(xn_out[:, kt * L:(kt + 1) * L], TMP[kt % 2], ar.f32("bcast2"),
             op=ALU.subtract).then_inc(V, 1)
    ar.free("xslot0", "xslot1", "lnrows", "lntmp", "lntmp2", "bcast1", "bcast2")
    return V.n




def emit_ffn(bx, T, ar, PSA, PSB, C, src_res, dst_res, also_out=False):
    t, v, s, y, g = bx.t, bx.v, bx.s, bx.y, bx.g
    PE, V, S, Y = bx.PE, bx.V, bx.S, bx.Y
    PSAa, PSBa = PSA.ap(), PSB.ap()

    ar.alloc("xn2", 8 * L)
    XN2 = ar.bf("xn2")
    ln_feature_major(bx, ar, PSA, PSB, C, T, T[src_res], XN2, 8)
    bx.barrier()

    ar.alloc("hslot", 32 * L)
    ar.alloc("fpan", 4 * 1024)
    ar.alloc("ft1", 2 * 512); ar.alloc("ft2", 2 * 512)
    H = ar.bf("hslot")
    FPAN = ar.bf("fpan")
    FT1, FT2 = ar.bf("ft1"), ar.bf("ft2")
    W1R = T["f_w1T"].rearrange("(kt p) m -> p kt m", p=128)
    W2R = T["f_w2T"].rearrange("(kt p) m -> p kt m", p=128)
    bx.g_sync()
    pe_m = [0] * 32
    ev_m = [0] * 32
    ftq = [0, 0]
    for m in range(32):
        sl = m % 2
        gs = bx.GS[sl]
        if m >= 2:
            g.wait(PE, pe_m[m - 2])
        g.dma(out=FPAN[:, sl * 2048: sl * 2048 + 1024].rearrange("p (kt c) -> p kt c", kt=8),
              in_=W1R[:, :, m * 128:(m + 1) * 128]).then_inc(gs, 16)
        g.dma(out=FPAN[:, sl * 2048 + 1024: sl * 2048 + 2048].rearrange("p (kt c) -> p kt c", kt=8),
              in_=W2R[:, :, m * 128:(m + 1) * 128]).then_inc(gs, 16)
        gm = gs.n
        t.wait(gs, gm)
        if m >= 1:
            t.wait(V, ev_m[m - 1])
        for k in range(8):
            for c in range(4):
                t.mm(PSAa[:, c * 512:(c + 1) * 512],
                     FPAN[:, sl * 2048 + k * 128: sl * 2048 + (k + 1) * 128],
                     XN2[:, k * L + c * 512: k * L + (c + 1) * 512],
                     start=(k == 0), stop=(k == 7))
        for k in range(8):
            for c in range(4):
                ins = t.mm(PSBa[:, c * 512:(c + 1) * 512],
                           FPAN[:, sl * 2048 + 1024 + k * 128: sl * 2048 + 1024 + (k + 1) * 128],
                           XN2[:, k * L + c * 512: k * L + (c + 1) * 512],
                           start=(k == 0), stop=(k == 7))
        ins.then_inc(PE, 1)
        pe_m[m] = PE.n
        s.wait(PE, pe_m[m])
        v.wait(PE, pe_m[m])
        for c in range(4):
            fsl = c % 2
            s.wait(V, ftq[fsl])
            s.act(FT1[:, fsl * 512:(fsl + 1) * 512], PSAa[:, c * 512:(c + 1) * 512], AF.Silu,
                  scale=C["s1"][:, m:m + 1]).then_inc(S, 1)
            v.wait(S, S.n)
            v.ts(FT2[:, fsl * 512:(fsl + 1) * 512], PSBa[:, c * 512:(c + 1) * 512],
                 C["s2"][:, m:m + 1], None, op0=ALU.mult)
            v.op("drain")
            v.tt(H[:, m * L + c * 512: m * L + (c + 1) * 512], FT1[:, fsl * 512:(fsl + 1) * 512],
                 FT2[:, fsl * 512:(fsl + 1) * 512], op=ALU.mult).then_inc(V, 1)
            ftq[fsl] = V.n
        ev_m[m] = V.n
    bx.barrier()
    ar.free("xn2", "fpan", "ft1", "ft2")

    ar.alloc("hpan", 2 * 4096)
    ar.alloc("otile", 2 * L)
    HPAN = ar.bf("hpan")
    OT = ar.bf("otile")
    for phase, wnm, snm, spnm in (("F2", "f_h1T", "s3", "sp_h1"), ("F3", "f_h2T", "s4", "sp_h2")):
        WR = T[wnm].rearrange("(kt p) m -> p kt m", p=128)
        if phase == "F3":
            g.wait(PE, PE.n)
            g.wait(bx.YS0, bx.YS0.n)
            rel_marks = []
            for kt in range(32):
                g.dma(out=H[:, kt * L:(kt + 1) * L],
                      in_=T["sp_h1"][kt * 128:(kt + 1) * 128, :]).then_inc(bx.GS2, 16)
                rel_marks.append(bx.GS2.n)
        pe_m = [0] * 32
        ev_m = [0] * 32
        ysp = [0] * 32
        for m in range(32):
            sl = m % 2
            gs = bx.GS[sl]
            if m >= 2:
                g.wait(PE, pe_m[m - 2])
            g.dma(out=HPAN[:, sl * 4096:(sl + 1) * 4096].rearrange("p (kt c) -> p kt c", kt=32),
                  in_=WR[:, :, m * 128:(m + 1) * 128]).then_inc(gs, 16)
            gm = gs.n
            t.wait(gs, gm)
            if phase == "F3" and m == 0:
                for mk in rel_marks:
                    t.wait(bx.GS2, mk)
            if m >= 2:
                t.wait(S, ev_m[m - 2])
            pset = PSAa if sl == 0 else PSBa
            for k in range(32):
                for c in range(4):
                    ins = t.mm(pset[:, c * 512:(c + 1) * 512],
                               HPAN[:, sl * 4096 + k * 128: sl * 4096 + (k + 1) * 128],
                               H[:, k * L + c * 512: k * L + (c + 1) * 512],
                               start=(k == 0), stop=(k == 31))
            ins.then_inc(PE, 1)
            pe_m[m] = PE.n
            s.wait(PE, pe_m[m])
            if m >= 2:
                s.wait(bx.YS0, ysp[m - 2])
            for c in range(4):
                ins = s.act(OT[:, sl * L + c * 512: sl * L + (c + 1) * 512],
                            pset[:, c * 512:(c + 1) * 512], AF.Gelu, scale=C[snm][:, m:m + 1])
            ins.then_inc(S, 1)
            ev_m[m] = S.n
            y.wait(S, ev_m[m])
            y.dma(out=T[spnm][m * 128:(m + 1) * 128, :], in_=OT[:, sl * L:(sl + 1) * L]).then_inc(bx.YS0, 16)
            ysp[m] = bx.YS0.n
        bx.barrier()

    g.wait(PE, PE.n)
    g.wait(bx.YS0, bx.YS0.n)
    rel_marks = []
    for kt in range(32):
        g.dma(out=H[:, kt * L:(kt + 1) * L],
              in_=T["sp_h2"][kt * 128:(kt + 1) * 128, :]).then_inc(bx.GS2, 16)
        rel_marks.append(bx.GS2.n)
    ar.free("hpan", "otile")
    ar.alloc("w3pan", 2 * 4096)
    ar.alloc("x1sf", 2 * 2048 * 2)
    ar.alloc("x2sf", 2 * 512 * 2)
    W3P = ar.bf("w3pan")
    X1F = ar.f32("x1sf")
    X2F = ar.f32("x2sf")
    W3R = T["f_w3T"].rearrange("(kt p) m -> p kt m", p=128)
    pe_m = [[0] * 4 for _ in range(8)]
    vx_m = [[0] * 4 for _ in range(8)]
    yx_m = [[0] * 4 for _ in range(8)]
    for m in range(8):
        sl = m % 2
        gs = bx.GS[sl]
        if m >= 2:
            g.wait(PE, pe_m[m - 2][3])
            g.wait(V, vx_m[m - 2][3])
        g.dma(out=W3P[:, sl * 4096:(sl + 1) * 4096].rearrange("p (kt c) -> p kt c", kt=32),
              in_=W3R[:, :, m * 128:(m + 1) * 128]).then_inc(gs, 16)
        g.dma(out=X1F[:, sl * 2048: sl * 2048 + 2048].rearrange("p (c q) -> p c q", c=4),
              in_=T[src_res][m * 128:(m + 1) * 128, :].rearrange("p (c q) -> p c q", c=4)).then_inc(gs, 16)
        gm = gs.n
        t.wait(gs, gm)
        if m == 0:
            for mk in rel_marks:
                t.wait(bx.GS2, mk)
        pset = PSAa if sl == 0 else PSBa
        for c in range(4):
            if m >= 1:
                t.wait(V, vx_m[m - 1][c])
            for k in range(32):
                ins = t.mm(pset[:, c * 512:(c + 1) * 512],
                           W3P[:, sl * 4096 + k * 128: sl * 4096 + (k + 1) * 128],
                           H[:, k * L + c * 512: k * L + (c + 1) * 512],
                           start=(k == 0), stop=(k == 31))
            ins.then_inc(PE, 1)
            pe_m[m][c] = PE.n
            v.wait(PE, pe_m[m][c])
            cs = c % 2
            if m * 4 + c >= 2:
                pm, pc = divmod(m * 4 + c - 2, 4)
                v.wait(bx.YS1, yx_m[pm][pc])
            v.stt(X2F[:, cs * 512:(cs + 1) * 512], pset[:, c * 512:(c + 1) * 512],
                  C["s5"][:, m:m + 1], X1F[:, sl * 2048 + c * 512: sl * 2048 + (c + 1) * 512],
                  op0=ALU.mult, op1=ALU.add).then_inc(V, 1)
            vx_m[m][c] = V.n
            y.wait(V, vx_m[m][c])
            y.dma(out=T[dst_res][m * 128:(m + 1) * 128, c * 512:(c + 1) * 512],
                  in_=X2F[:, cs * 512:(cs + 1) * 512]).then_inc(bx.YS1, 16)
            if also_out:
                y.dma(out=T["out"][m * 128:(m + 1) * 128, c * 512:(c + 1) * 512],
                      in_=X2F[:, cs * 512:(cs + 1) * 512]).then_inc(bx.YS1, 16)
            yx_m[m][c] = bx.YS1.n
    bx.barrier()
    ar.free("hslot", "w3pan", "x1sf", "x2sf")




def emit_attn_b(bx, T, ar, PSA, PSB, C, upto="attn_b"):
    t, v, s, y, g = bx.t, bx.v, bx.s, bx.y, bx.g
    PE, V, S, Y = bx.PE, bx.V, bx.S, bx.Y
    PSAa, PSBa = PSA.ap(), PSB.ap()
    NCH, CH, NH, HD = 16, 128, 32, 64

    ar.alloc("xn3", 8 * L)
    XN3 = ar.bf("xn3")
    ln_feature_major(bx, ar, PSA, PSB, C, T, T["xr2"], XN3, 8)
    bx.barrier()

    ar.alloc("xbc", 17 * 2052)
    ar.alloc("xbc_bc", 2052)
    ar.alloc("bpan", 3 * 1024)
    ar.alloc("bzt", 2 * L)
    ar.alloc("bdtr", 2 * L * 2)
    XBC = ar.bf("xbc")
    XBCB = ar.bf("xbc_bc")
    BPAN = ar.bf("bpan")
    BZT = ar.bf("bzt")
    BDTR = ar.f32("bdtr")
    INR = T["b_inT"].rearrange("(kt p) m -> p kt m", p=128)
    bx.g_sync()
    pe_m = [0] * 34
    ev_m = [0] * 34
    yz_m = [0] * 34
    for m in range(34):
        sl = m % 3
        gs = bx.GS[sl]
        mw = 128 if m < 33 else 32
        if m >= 3:
            g.wait(PE, pe_m[m - 3])
        g.dma(out=BPAN[:, sl * 1024: sl * 1024 + 8 * mw].rearrange("p (kt c) -> p kt c", kt=8),
              in_=INR[:, :, m * 128: m * 128 + mw]).then_inc(gs, 16)
        t.wait(gs, gs.n)
        if m >= 2:
            t.wait(S, ev_m[m - 2])
        pset = PSAa if (m % 2 == 0) else PSBa
        ins = None
        for k in range(8):
            for c in range(4):
                ins = t.mm(pset[0:mw, c * 512:(c + 1) * 512],
                           BPAN[:, sl * 1024 + k * mw: sl * 1024 + (k + 1) * mw],
                           XN3[:, k * L + c * 512: k * L + (c + 1) * 512],
                           start=(k == 0), stop=(k == 7))
        ins.then_inc(PE, 1)
        pe_m[m] = PE.n
        s.wait(PE, pe_m[m])
        if m < 16:
            zslot = m % 2
            ys = bx.YS[zslot]
            if m >= 2:
                s.wait(ys, yz_m[m - 2])
            ins = None
            for c in range(4):
                ins = s.act(BZT[:, zslot * L + c * 512: zslot * L + (c + 1) * 512],
                            pset[:, c * 512:(c + 1) * 512], AF.Silu)
            ins.then_inc(S, 1)
            ev_m[m] = S.n
            y.wait(S, ev_m[m])
            y.dma(out=T["sp_zs"][m * 128:(m + 1) * 128, :],
                  in_=BZT[:, zslot * L:(zslot + 1) * L]).then_inc(ys, 16)
            yz_m[m] = ys.n
        elif m < 33:
            ct = m - 16
            dst = XBC if ct < 16 else XBCB
            base = ct * 2052 + 3 if ct < 16 else 3
            ins = None
            for c in range(4):
                ins = s.act(dst[:, base + c * 512: base + (c + 1) * 512],
                            pset[:, c * 512:(c + 1) * 512], AF.Copy)
            ins.then_inc(S, 1)
            ev_m[m] = S.n
        else:
            ins = None
            for c in range(4):
                ins = s.act(BDTR[0:32, c * 512:(c + 1) * 512],
                            pset[0:32, c * 512:(c + 1) * 512], AF.Copy)
            ins.then_inc(S, 1)
            ev_m[m] = S.n
    bx.barrier()
    ar.free("bpan", "bzt")
    if upto == "b_proj":
        return

    ar.alloc("cacc", L * 2 * 2)
    CACCB = ar.f32("cacc")
    smk = [0] * 17
    v.memset(XBCB[:, 0:3], 0.0)
    for ct in range(17):
        buf = XBC if ct < 16 else XBCB
        base = ct * 2052 if ct < 16 else 0
        CACC = CACCB[:, (ct % 2) * L:(ct % 2) * L + L]
        if ct >= 2:
            v.wait(S, smk[ct - 2])
        if ct < 16:
            v.memset(buf[:, base:base + 3], 0.0)
        v.op("drain")
        v.ts(CACC[:, 0:L], buf[:, base + 3: base + 3 + L], C["cw"][:, ct * 4 + 3: ct * 4 + 4],
             None, op0=ALU.mult)
        for kk in range(3):
            v.op("drain")
            v.stt(CACC[:, 0:L], buf[:, base + kk: base + kk + L],
                  C["cw"][:, ct * 4 + kk: ct * 4 + kk + 1],
                  CACC[:, 0:L], op0=ALU.mult, op1=ALU.add)
        v.op("drain").then_inc(V, 1)
        s.wait(V, V.n)
        s.act(buf[:, base + 3: base + 3 + L], CACC[:, 0:L], AF.Silu,
              bias=C["cb"][:, ct:ct + 1]).then_inc(S, 1)
        smk[ct] = S.n
    ar.alloc("xbcc", 2052)
    XBCC = ar.bf("xbcc")
    y.wait(S, S.n)
    y.dma(out=XBCC[0:64, :], in_=XBCB[64:128, :]).then_inc(Y, 16)
    bx.barrier()
    ar.free("cacc")
    if upto == "b_conv":
        return

    ar.alloc("brow2", 2 * L * 2)
    BW = ar.f32("brow2")
    sS = BW[0:32, 0:L]
    WR_ = BW[0:32, L:2 * L]
    s.act(BDTR[0:32, L:2 * L], BDTR[0:32, 0:L], AF.Abs, bias=C["bdtb"][:])
    s.act(BDTR[0:32, L:2 * L], BDTR[0:32, L:2 * L], AF.Exp, scale=-1.0)
    s.act(BDTR[0:32, L:2 * L], BDTR[0:32, L:2 * L], AF.Ln, bias=1.0)
    s.act(BDTR[0:32, 0:L], BDTR[0:32, 0:L], AF.Relu, bias=C["bdtb"][:])
    s.op("drain").then_inc(S, 1)
    v.wait(S, S.n)
    v.tt(BDTR[0:32, 0:L], BDTR[0:32, 0:L], BDTR[0:32, L:2 * L], op=ALU.add)
    v.op("drain")
    v.ts(sS, BDTR[0:32, 0:L], C["bAcol"][:], None, op0=ALU.mult)
    v.memset(WR_, 1.0)
    v.op("drain")
    v.memset(BW[0:32, L:2 * L].rearrange("p (c q) -> p c q", c=16)[:, :, 0:1], 0.0)
    v.op("drain")
    v.scan(sS, WR_, sS)
    v.op("drain").then_inc(V, 1)
    y.wait(V, V.n)
    y.dma(out=T["sp_es"][:, :], in_=sS).then_inc(Y, 16)
    v.ts(WR_, sS, -1.0, None, op0=ALU.mult)
    v.op("drain").then_inc(V, 1)
    y.wait(V, V.n)
    y.dma(out=T["sp_esn"][:, :], in_=WR_).then_inc(Y, 16)
    sv = sS.rearrange("p (c q) -> p c q", c=16)
    send = sv[:, :, 127:128].broadcast_to([32, 16, CH])
    v.tt(BW[0:32, L:2 * L].rearrange("p (c q) -> p c q", c=16), send, sv, op=ALU.subtract)
    v.op("drain").then_inc(V, 1)
    s.wait(V, V.n)
    s.act(WR_, WR_, AF.Exp).then_inc(S, 1)
    v.wait(S, S.n)
    v.tt(WR_, WR_, BDTR[0:32, 0:L], op=ALU.mult)
    v.op("drain").then_inc(V, 1)
    bx.barrier()

    ar.alloc("tcols", 4 * 512 * 2)
    TC = ar.f32("tcols")
    dtT = TC[:, 0:512]
    WT = TC[:, 512:1024]
    sT = TC[:, 1024:1536]
    EST = TC[:, 1536:2048]
    PSTb = PSBa[:, 1536:2048].bitcast(BF16)
    PSTf = PSAa[:, 1536:2048]
    for qi, srcrow in enumerate((BDTR[0:32, 0:L], WR_, sS)):
        for k in range(NCH):
            t.op("transpose", PSTf[:, 0:32], srcrow[:, k * CH:(k + 1) * CH],
                 C["identF"][0:32, 0:32]).then_inc(PE, 1)
            v.wait(PE, PE.n)
            v.copy(TC[:, qi * 512 + k * 32: qi * 512 + (k + 1) * 32], PSTf[:, 0:32])
            v.op("drain").then_inc(V, 1)
            t.wait(V, V.n)
    s.wait(V, V.n)
    s.act(EST, sT, AF.Exp).then_inc(S, 1)
    mark_est = S.n
    bx.barrier()
    ar.free("bdtr", "brow2")
    ar.alloc("xst", 16 * L)
    XST = ar.bf("xst")
    def tbank(s):
        base = PSAa if s < 4 else PSBa
        return base[:, (s % 4) * 512:(s % 4) * 512 + 64].bitcast(BF16)
    evmk = [0] * 8
    pscnt = 0
    for ct in range(16):
        for k in range(NCH):
            sl8 = pscnt % 8
            if pscnt >= 8:
                t.wait(V, evmk[sl8])
            t.op("transpose", tbank(sl8),
                 XBC[:, ct * 2052 + 3 + k * CH: ct * 2052 + 3 + (k + 1) * CH],
                 C["ident"][:]).then_inc(PE, 1)
            v.wait(PE, PE.n)
            v.copy(XST[:, k * L + ct * 128: k * L + (ct + 1) * 128], tbank(sl8))
            v.op("drain").then_inc(V, 1)
            evmk[sl8] = V.n
            pscnt += 1
    bx.barrier()
    ar.free("xbc")
    if upto == "b_trans":
        return

    ar.alloc("srow", 4096 * 2 + 64)
    ar.alloc("sbc", 4096 * 2)
    ar.alloc("gm", 128)
    ar.alloc("mgt", 2 * 128)
    ar.alloc("dsb", 2 * 256)
    ar.alloc("btok", 64)
    ar.alloc("bht", 2 * 64)
    ar.alloc("hst0", NH * HD)
    ar.alloc("hst1", NH * HD)
    ar.alloc("decb", 64)
    ar.alloc("tri", 128)
    ar.alloc("ytile", L)
    ar.alloc("ydump", L)
    ar.alloc("ycols", 64)
    ar.alloc("zst", L)
    ar.alloc("ynstg", L)
    ar.alloc("yevt", 2 * 128)
    SROW = ar.f32("srow")
    SBC = ar.f32("sbc")
    GM = ar.bf("gm")
    MGT = ar.bf("mgt")
    DSB = ar.f32("dsb")
    BTOK = ar.bf("btok")
    BHT = ar.bf("bht")
    HST = [ar.bf("hst0"), ar.bf("hst1")]
    DECB = ar.f32("decb")
    TRI = ar.bf("tri")
    YT = ar.bf("ytile")
    YDUMP = ar.bf("ydump")
    YCOL = ar.f32("ycols")
    ZST = ar.bf("zst")
    YNS = ar.bf("ynstg")
    YEVT = ar.f32("yevt")
    ynsp = [0] * 16
    v.ts(TRI[:], C["iot"][:], 0.0, None, op0=ALU.is_ge)
    v.memset(HST[0][0:64, :], 0.0)
    v.op("drain").then_inc(V, 1)
    mark_init = V.n
    g.wait(V, mark_init)
    zev = [0] * 16
    sq_mark = 0
    for k in range(NCH):
        y.wait(V, V.n)
        y.wait(S, S.n)
        y.dma(out=SROW[0:1, 0:4096].rearrange("p (h c) -> p h c", h=32),
              in_=T["sp_es"][:, k * CH:(k + 1) * CH].rearrange("h c -> () h c")).then_inc(Y, 16)
        y.wait(Y, Y.n)
        y.dma(out=SBC[:, 0:4096].rearrange("p (h c) -> p h c", h=32),
              in_=T["sp_es"][:, k * CH:(k + 1) * CH].rearrange("h c -> () h c").partition_broadcast(128)).then_inc(Y, 16)
        mark_srow = Y.n
        gs = bx.GS[k % 2]
        if k >= 1:
            g.wait(S, sq_mark)
        g.dma(out=YDUMP[:, 0:2048].rearrange("p (ct c) -> p ct c", ct=16),
              in_=T["sp_zs"].rearrange("(ct p) l -> p ct l", p=128)[:, :, k * CH:(k + 1) * CH]).then_inc(gs, 16)
        mark_zld = gs.n
        t.wait(gs, mark_zld)
        if k >= 1:
            t.wait(V, V.n)
        for ct in range(16):
            t.op("transpose", PSTb[:, (ct % 8) * 128:((ct % 8) + 1) * 128],
                 YDUMP[:, ct * 128:(ct + 1) * 128], C["ident"][:]).then_inc(PE, 1)
            v.wait(PE, PE.n)
            v.copy(ZST[:, ct * 128:(ct + 1) * 128], PSTb[:, (ct % 8) * 128:((ct % 8) + 1) * 128])
            v.op("drain").then_inc(V, 1)
            t.wait(V, V.n)
        t.mm(PSAa[:, 0:128], XBCB[0:64, 3 + k * CH: 3 + (k + 1) * CH],
             XBCC[0:64, 3 + k * CH: 3 + (k + 1) * CH], start=True, stop=True).then_inc(PE, 1)
        mark_g = PE.n
        t.op("transpose", PSTb[:, 0:64], XBCB[0:64, 3 + k * CH: 3 + (k + 1) * CH],
             C["ident"][0:64, 0:64]).then_inc(PE, 1)
        mark_btk = PE.n
        v.wait(PE, mark_g)
        v.tt(GM[:, 0:128], PSAa[:, 0:128], TRI[:], op=ALU.mult)
        v.op("drain").then_inc(V, 1)
        mark_gm = V.n
        v.wait(PE, mark_btk)
        v.copy(BTOK[:, 0:64], PSTb[:, 0:64])
        v.op("drain").then_inc(V, 1)
        mark_btok = V.n
        s.wait(Y, mark_srow)
        s.act(SROW[0:1, 4096:4128].rearrange("p c -> p c ()"),
              SROW[0:1, 0:4096].rearrange("p (h c) -> p h c", h=32)[:, :, 127:128],
              AF.Exp)
        s.op("drain").then_inc(S, 1)
        mark_erow = S.n
        t.wait(S, mark_erow)
        t.mm(PSAa[0:64, 1536:1536 + 32], C["onesrow"][0:1, 0:64],
             SROW[0:1, 4096:4128], start=True, stop=True).then_inc(PE, 1)
        v.wait(PE, PE.n)
        v.copy(DECB[0:64, 0:32], PSAa[0:64, 1536:1536 + 32])
        v.op("drain").then_inc(V, 1)
        mark_dec = V.n
        for h in range(NH):
            sl = h % 2
            v.wait(Y, mark_srow)
            v.ts(DSB[:, sl * 128:(sl + 1) * 128], SBC[:, h * CH:(h + 1) * CH],
                 sT[:, k * 32 + h: k * 32 + h + 1], 0.0, op0=ALU.subtract, op1=ALU.min)
            v.op("drain").then_inc(V, 1)
            mark_d = V.n
            s.wait(V, mark_d)
            s.act(DSB[:, sl * 128:(sl + 1) * 128], DSB[:, sl * 128:(sl + 1) * 128],
                  AF.Exp).then_inc(S, 1)
            mark_e = S.n
            v.wait(S, mark_e)
            v.wait(V, mark_gm)
            v.stt(MGT[:, sl * 128:(sl + 1) * 128], DSB[:, sl * 128:(sl + 1) * 128],
                  dtT[:, k * 32 + h: k * 32 + h + 1], GM[:, 0:128],
                  op0=ALU.mult, op1=ALU.mult)
            v.op("drain").then_inc(V, 1)
            mark_mg = V.n
            v.wait(V, mark_btok)
            v.ts(BHT[:, sl * 64:(sl + 1) * 64], BTOK[:, 0:64],
                 WT[:, k * 32 + h: k * 32 + h + 1], None, op0=ALU.mult)
            v.op("drain").then_inc(V, 1)
            mark_bh = V.n
            psY = PSAa[:, sl * 512: sl * 512 + 64]
            psY2 = PSAa[:, 1024 + sl * 256: 1024 + sl * 256 + 64]
            psH = PSBa[0:64, sl * 512: sl * 512 + 64]
            t.wait(V, mark_mg)
            t.mm(psY, MGT[:, sl * 128:(sl + 1) * 128],
                 XST[:, k * L + h * HD: k * L + (h + 1) * HD], start=True, stop=True)
            t.mm(psY2, XBCC[0:64, 3 + k * CH: 3 + (k + 1) * CH],
                 HST[k % 2][0:64, h * HD:(h + 1) * HD], start=True, stop=True).then_inc(PE, 1)
            mark_y = PE.n
            t.wait(V, mark_bh)
            t.mm(psH, BHT[:, sl * 64:(sl + 1) * 64],
                 XST[:, k * L + h * HD: k * L + (h + 1) * HD], start=True, stop=True).then_inc(PE, 1)
            mark_hu = PE.n
            v.wait(PE, mark_y)
            v.ts(YEVT[:, sl * 64: sl * 64 + 64], psY2,
                 EST[:, k * 32 + h: k * 32 + h + 1], None, op0=ALU.mult)
            v.stt(YT[:, h * HD:(h + 1) * HD], XST[:, k * L + h * HD: k * L + (h + 1) * HD],
                  C["Db"][0:128, h:h + 1], psY, op0=ALU.mult, op1=ALU.add)
            v.op("drain")
            v.tt(YT[:, h * HD:(h + 1) * HD], YT[:, h * HD:(h + 1) * HD],
                 YEVT[:, sl * 64: sl * 64 + 64], op=ALU.add)
            v.op("drain").then_inc(V, 1)
            v.wait(PE, mark_hu)
            if k == 0:
                v.wait(V, mark_init)
            v.stt(HST[(k + 1) % 2][0:64, h * HD:(h + 1) * HD],
                  HST[k % 2][0:64, h * HD:(h + 1) * HD],
                  DECB[0:64, h:h + 1], psH, op0=ALU.mult, op1=ALU.add)
            v.op("drain").then_inc(V, 1)
        v.tt(YT[:, 0:L], YT[:, 0:L], ZST[:, 0:L], op=ALU.mult)
        v.op("drain").then_inc(V, 1)
        s.wait(V, V.n)
        s.op("activation", out=YDUMP[:, 0:L], in_=YT[:, 0:L], func=AF.Square,
             accum_out=YCOL[:, 0:1])
        s.op("drain").then_inc(S, 1)
        sq_mark = S.n
        s.act(YCOL[:, 2:3], YCOL[:, 0:1], AF.Ln, scale=1.0 / E, bias=LN_EPS)
        s.act(YCOL[:, 2:3], YCOL[:, 2:3], AF.Exp, scale=-0.5)
        s.op("drain").then_inc(S, 1)
        v.wait(S, S.n)
        v.stt(YT[:, 0:L], YT[:, 0:L], YCOL[:, 2:3], C["nwb"][:, 0:L], op0=ALU.mult, op1=ALU.mult)
        v.op("drain").then_inc(V, 1)
        mark_yn = V.n
        if k >= 2:
            v.wait(bx.YS1, ynsp[k - 2]) if False else None
        for ct in range(16):
            t.wait(V, mark_yn if ct == 0 else V.n)
            t.op("transpose", PSTb[:, (ct % 8) * 128:((ct % 8) + 1) * 128],
                 YT[:, ct * 128:(ct + 1) * 128], C["ident"][:]).then_inc(PE, 1)
            v.wait(PE, PE.n)
            if ct == 0 and k >= 1:
                v.wait(bx.YS1, ynsp[k - 1])
            v.copy(YNS[:, ct * 128:(ct + 1) * 128],
                   PSTb[:, (ct % 8) * 128:((ct % 8) + 1) * 128])
            v.op("drain").then_inc(V, 1)
        y.wait(V, V.n)
        y.dma(out=T["sp_yz"].rearrange("(ct p) l -> p ct l", p=128)[:, :, k * 128:(k + 1) * 128],
              in_=YNS[:, 0:2048].rearrange("p (ct c) -> p ct c", ct=16)).then_inc(bx.YS1, 16)
        ynsp[k] = bx.YS1.n
    bx.barrier()
    ar.free("srow", "sbc", "gm", "mgt", "dsb", "btok", "bht", "hst0", "hst1",
            "decb", "tri", "ytile", "ydump", "zst", "xst", "tcols", "xbc_bc", "ynstg", "xbcc",
            "ycols", "yevt")
    if upto == "b_scan":
        return

    ar.alloc("ybuf", 16 * L)
    YBUF = ar.bf("ybuf")
    g.wait(bx.YS1, bx.YS1.n)
    bx.g_sync()
    for kt in range(16):
        g.dma(out=YBUF[:, kt * L:(kt + 1) * L],
              in_=T["sp_yz"][kt * 128:(kt + 1) * 128, :]).then_inc(bx.GS2, 16)
    mark_ybr = bx.GS2.n
    ar.alloc("bout", 16 * 1024)
    ar.alloc("xbt", 2 * 512 * 2)
    ar.alloc("sqs", 2 * 512 * 2)
    ar.alloc("rsr", L * 2)
    BOUT = ar.bf("bout").rearrange("p (kt m) -> p kt m", kt=16)
    XBT = ar.f32("xbt")
    SQS = ar.f32("sqs")
    RSR = ar.f32("rsr")
    BOUT_SRC = T["b_outT"].rearrange("(kt p) m -> p kt m", p=128)
    for kt in range(16):
        g.dma(out=BOUT[:, kt:kt + 1, :], in_=BOUT_SRC[:, kt:kt + 1, :]).then_inc(bx.GS2, 16)
    t.wait(bx.GS2, bx.GS2.n)
    pv = [[0] * 8 for _ in range(4)]
    sqpe = [0, 0]
    vx = [[0] * 8 for _ in range(4)]
    yx = [[0] * 8 for _ in range(4)]
    sq = [[0] * 8 for _ in range(4)]
    for lc in range(4):
        for d_ in range(8):
            psd = [PSAa[:, 0:512], PSAa[:, 512:1024], PSAa[:, 1024:1536],
                   PSBa[:, 0:512], PSBa[:, 512:1024], PSBa[:, 1024:1536]][(lc * 8 + d_) % 6]
            if lc * 8 + d_ >= 6:
                pd, dd = divmod(lc * 8 + d_ - 6, 8)
                t.wait(V, vx[pd][dd])
            ins = None
            for kt in range(16):
                ins = t.mm(psd, BOUT[:, kt, d_ * 128:(d_ + 1) * 128],
                           YBUF[:, kt * L + lc * 512: kt * L + lc * 512 + 512],
                           start=(kt == 0), stop=(kt == 15))
            ins.then_inc(PE, 1)
            pv[lc][d_] = PE.n
            v.wait(PE, pv[lc][d_])
            cs = d_ % 2
            if lc * 8 + d_ >= 2:
                pd, dd = divmod(lc * 8 + d_ - 2, 8)
                v.wait(bx.YS0, yx[pd][dd])
                v.wait(S, sq[pd][dd])
            v.tt(XBT[:, cs * 512:(cs + 1) * 512], psd,
                 XN3[:, d_ * L + lc * 512: d_ * L + lc * 512 + 512], op=ALU.add)
            v.op("drain").then_inc(V, 1)
            vx[lc][d_] = V.n
            y.wait(V, vx[lc][d_])
            y.dma(out=T["sp_xb"][d_ * 128:(d_ + 1) * 128, lc * 512:(lc + 1) * 512],
                  in_=XBT[:, cs * 512:(cs + 1) * 512]).then_inc(bx.YS0, 16)
            yx[lc][d_] = bx.YS0.n
            s.wait(V, vx[lc][d_])
            s.wait(PE, sqpe[d_ % 2])
            s.act(SQS[:, cs * 512:(cs + 1) * 512], XBT[:, cs * 512:(cs + 1) * 512], AF.Square)
            s.op("drain").then_inc(S, 1)
            sq[lc][d_] = S.n
            t.wait(S, sq[lc][d_])
            t.mm(PSBa[0:1, 1536:2048], C["ones_f"][:], SQS[:, cs * 512:(cs + 1) * 512],
                 start=(d_ == 0), stop=(d_ == 7)).then_inc(PE, 1)
            sqpe[d_ % 2] = PE.n
        s.wait(PE, PE.n)
        s.act(RSR[0:1, lc * 512:(lc + 1) * 512], PSBa[0:1, 1536:2048], AF.Ln,
              scale=1.0 / D, bias=LN_EPS)
        s.act(RSR[0:1, lc * 512:(lc + 1) * 512], RSR[0:1, lc * 512:(lc + 1) * 512],
              AF.Exp, scale=-0.5)
        s.op("drain").then_inc(S, 1)
        t.wait(S, S.n)
    mark_xbdone = bx.YS0.n
    bx.barrier()
    ar.free("bout", "xbt", "sqs", "ybuf", "xn3")
    if upto == "b_out":
        return

    ar.alloc("rsb", L * 2)
    ar.alloc("xnr", 8 * L)
    RSB = ar.f32("rsb")
    XNR = ar.bf("xnr")
    y.dma(out=T["sp_rows"][0:1, :], in_=RSR[0:1, 0:L]).then_inc(Y, 16)
    y.wait(Y, Y.n)
    y.dma(out=RSB, in_=T["sp_rows"][0:1, :].partition_broadcast(128)).then_inc(Y, 16)
    mark_rsb = Y.n
    ar.free("rsr")
    bx.g_sync()
    g.wait(bx.YS0, mark_xbdone)
    gm_ = [0] * 8
    ar.alloc("xbstg", 2 * L * 2)
    XBS = ar.f32("xbstg")
    for d_ in range(8):
        gs = bx.GS[d_ % 2]
        if d_ >= 2:
            g.wait(V, gm_[d_ - 2])
        g.dma(out=XBS[:, (d_ % 2) * L:((d_ % 2) + 1) * L],
              in_=T["sp_xb"][d_ * 128:(d_ + 1) * 128, :]).then_inc(gs, 16)
        v.wait(gs, gs.n)
        v.wait(Y, mark_rsb)
        v.stt(XNR[:, d_ * L:(d_ + 1) * L], XBS[:, (d_ % 2) * L:((d_ % 2) + 1) * L],
              C["brmsw"][:, d_:d_ + 1], RSB, op0=ALU.mult, op1=ALU.mult)
        v.op("drain").then_inc(V, 1)
        gm_[d_] = V.n
    bx.barrier()
    ar.free("xbstg")
    ar.alloc("hslot", 32 * L)
    ar.alloc("fpan", 4 * 1024)
    ar.alloc("ft1", 2 * 512)
    H_ = ar.bf("hslot")
    FPAN = ar.bf("fpan")
    FT1 = ar.bf("ft1")
    F1R = T["b_fc1T"].rearrange("(kt p) m -> p kt m", p=128)
    pe_m = [0] * 32
    ev_m = [0] * 32
    ftq = [0, 0]
    for m in range(32):
        sl = m % 2
        gs = bx.GS[sl]
        if m >= 2:
            g.wait(PE, pe_m[m - 2])
        g.dma(out=FPAN[:, sl * 2048: sl * 2048 + 1024].rearrange("p (kt c) -> p kt c", kt=8),
              in_=F1R[:, :, m * 128:(m + 1) * 128]).then_inc(gs, 16)
        g.dma(out=FPAN[:, sl * 2048 + 1024: sl * 2048 + 2048].rearrange("p (kt c) -> p kt c", kt=8),
              in_=F1R[:, :, 4096 + m * 128: 4096 + (m + 1) * 128]).then_inc(gs, 16)
        t.wait(gs, gs.n)
        if m >= 1:
            t.wait(V, ev_m[m - 1])
        for k in range(8):
            for c in range(4):
                t.mm(PSAa[:, c * 512:(c + 1) * 512],
                     FPAN[:, sl * 2048 + k * 128: sl * 2048 + (k + 1) * 128],
                     XNR[:, k * L + c * 512: k * L + (c + 1) * 512],
                     start=(k == 0), stop=(k == 7))
        for k in range(8):
            for c in range(4):
                ins = t.mm(PSBa[:, c * 512:(c + 1) * 512],
                           FPAN[:, sl * 2048 + 1024 + k * 128: sl * 2048 + 1024 + (k + 1) * 128],
                           XNR[:, k * L + c * 512: k * L + (c + 1) * 512],
                           start=(k == 0), stop=(k == 7))
        ins.then_inc(PE, 1)
        pe_m[m] = PE.n
        s.wait(PE, pe_m[m])
        v.wait(PE, pe_m[m])
        for c in range(4):
            fsl = c % 2
            s.wait(V, ftq[fsl])
            s.act(FT1[:, fsl * 512:(fsl + 1) * 512], PSBa[:, c * 512:(c + 1) * 512],
                  AF.Silu).then_inc(S, 1)
            v.wait(S, S.n)
            v.tt(H_[:, m * L + c * 512: m * L + (c + 1) * 512],
                 PSAa[:, c * 512:(c + 1) * 512], FT1[:, fsl * 512:(fsl + 1) * 512], op=ALU.mult)
            v.op("drain").then_inc(V, 1)
            ftq[fsl] = V.n
        ev_m[m] = V.n
    bx.barrier()
    ar.free("xnr", "fpan", "ft1")
    if upto == "b_fc1":
        return

    ar.alloc("w3pan", 2 * 4096)
    ar.alloc("x1sf", 2 * 2048 * 2)
    ar.alloc("xbsf", 2 * 2048 * 2)
    ar.alloc("x2sf", 2 * 512 * 2)
    bx.g_sync()
    W3P = ar.bf("w3pan")
    X1F = ar.f32("x1sf")
    XBF = ar.f32("xbsf")
    X2F = ar.f32("x2sf")
    F2R = T["b_fc2T"].rearrange("(kt p) m -> p kt m", p=128)
    pe_mq = [[0] * 4 for _ in range(8)]
    vx_m = [[0] * 4 for _ in range(8)]
    yx_m = [[0] * 4 for _ in range(8)]
    for m in range(8):
        sl = m % 2
        gs = bx.GS[sl]
        if m >= 2:
            g.wait(PE, pe_mq[m - 2][3])
            g.wait(V, vx_m[m - 2][3])
        g.dma(out=W3P[:, sl * 4096:(sl + 1) * 4096].rearrange("p (kt c) -> p kt c", kt=32),
              in_=F2R[:, :, m * 128:(m + 1) * 128]).then_inc(gs, 16)
        g.dma(out=X1F[:, sl * 2048: sl * 2048 + 2048],
              in_=T["xr2"][m * 128:(m + 1) * 128, :]).then_inc(gs, 16)
        g.dma(out=XBF[:, sl * 2048: sl * 2048 + 2048],
              in_=T["sp_xb"][m * 128:(m + 1) * 128, :]).then_inc(gs, 16)
        t.wait(gs, gs.n)
        pset = PSAa if sl == 0 else PSBa
        for c in range(4):
            if m >= 1:
                t.wait(V, vx_m[m - 1][c])
            for k in range(32):
                ins = t.mm(pset[:, c * 512:(c + 1) * 512],
                           W3P[:, sl * 4096 + k * 128: sl * 4096 + (k + 1) * 128],
                           H_[:, k * L + c * 512: k * L + (c + 1) * 512],
                           start=(k == 0), stop=(k == 31))
            ins.then_inc(PE, 1)
            pe_mq[m][c] = PE.n
            v.wait(PE, pe_mq[m][c])
            cs = c % 2
            if m * 4 + c >= 2:
                pm, pc = divmod(m * 4 + c - 2, 4)
                v.wait(bx.YS1, yx_m[pm][pc])
            v.tt(X2F[:, cs * 512:(cs + 1) * 512], pset[:, c * 512:(c + 1) * 512],
                 XBF[:, sl * 2048 + c * 512: sl * 2048 + (c + 1) * 512], op=ALU.add)
            v.op("drain")
            v.tt(X2F[:, cs * 512:(cs + 1) * 512], X2F[:, cs * 512:(cs + 1) * 512],
                 X1F[:, sl * 2048 + c * 512: sl * 2048 + (c + 1) * 512], op=ALU.add)
            v.op("drain").then_inc(V, 1)
            vx_m[m][c] = V.n
            y.wait(V, vx_m[m][c])
            y.dma(out=T["xr3"][m * 128:(m + 1) * 128, c * 512:(c + 1) * 512],
                  in_=X2F[:, cs * 512:(cs + 1) * 512]).then_inc(bx.YS1, 16)
            if upto == "attn_b":
                y.dma(out=T["out"][m * 128:(m + 1) * 128, c * 512:(c + 1) * 512],
                      in_=X2F[:, cs * 512:(cs + 1) * 512]).then_inc(bx.YS1, 16)
            yx_m[m][c] = bx.YS1.n
    bx.barrier()
    ar.free("hslot", "w3pan", "x1sf", "xbsf", "x2sf", "rsb")


def emit_module(bx, T, TAPS, ar, PSA, PSB, C, upto="attn_a"):
    t, v, s, y, g = bx.t, bx.v, bx.s, bx.y, bx.g
    GP = bx.GP

    def finish_early():
        y.dma(out=T["out"], in_=T["xr0"]).then_inc(bx.Y, 16)
        y.wait(bx.G, bx.G.n)
        y.wait(bx.Y, bx.Y.n)
        for sm in (bx.GS0, bx.GS1, bx.GS2, bx.YS0, bx.YS1, bx.GP):
            y.wait(sm, sm.n)
        bx.barrier()
    PE, V, S, Y, G = bx.PE, bx.V, bx.S, bx.Y, bx.G
    PSAa, PSBa = PSA.ap(), PSB.ap()

    def psa(c):
        return PSAa[:, c * 512:(c + 1) * 512]

    def psb(c):
        return PSBa[:, c * 512:(c + 1) * 512]

    g.op("iota", C["iot"][:], pattern=[[1, 128]], base=0, channel_multiplier=-1,
         allow_small_or_imprecise_dtypes=True).then_inc(GP, 1)
    mark_iota = GP.n
    const_pairs = [("a_Alog", "cAlog"), ("a_bdt", "bdt"), ("a_Da", "Da"),
                   ("f_s1", "s1"), ("f_s2", "s2"), ("f_s3", "s3"), ("f_s4", "s4"), ("f_s5", "s5"),
                   ("b_cw", "cw"), ("b_cb", "cb"), ("b_dtb", "bdtb"), ("b_rmsw", "brmsw")]
    for q in ("Be", "Bo", "Ce", "Co"):
        const_pairs += [("a_bias" + q, "bias" + q), ("a_norm" + q, "norm" + q)]
    for name, dst in const_pairs:
        y.dma(out=C[dst][:], in_=T[name]).then_inc(Y, 16)
    y.dma(out=C["mini"][0:1, 0:8], in_=T["a_freq_row"]).then_inc(Y, 16)
    mark_consts = Y.n
    y.dma(out=C["Db"][:], in_=T["b_D_row"].partition_broadcast(128)).then_inc(Y, 16)
    y.dma(out=C["nwb"][:], in_=T["b_nw_row"].partition_broadcast(128)).then_inc(Y, 16)
    y.dma(out=C["bAcol"][:], in_=T["b_Alog"]).then_inc(Y, 16)
    mark_consts = Y.n
    v.wait(GP, mark_iota)
    v.ts(C["ident"][:], C["iot"][:], 0.0, None, op0=ALU.is_equal)
    v.ts(C["identF"][:], C["iot"][:], 0.0, None, op0=ALU.is_equal)
    v.memset(C["onesrow"][:], 1.0)
    v.memset(C["ones_f"][:], 1.0)
    v.memset(C["cvals"][:, 0:1], LN_EPS)
    v.memset(C["cvals"][:, 1:2], RMS_EPS_A)
    v.memset(C["ones_b"][:], 1.0)
    s.wait(Y, mark_consts)
    s.act(C["A"][:], C["cAlog"][:], AF.Exp)
    s.act(C["bAcol"][:], C["bAcol"][:], AF.Exp)
    bx.barrier()
    v.ts(C["bAcol"][:], C["bAcol"][:], -1.0, None, op0=ALU.mult)
    v.ts(C["A"][:], C["A"][:], 1e-4, -1.0, op0=ALU.max, op1=ALU.mult)
    bx.barrier()
    if upto == "setup":
        finish_early()
        return

    ar.alloc("xn1", 8 * L)
    XN1 = ar.bf("xn1")
    ln_feature_major(bx, ar, PSA, PSB, C, T, T["xr0"], XN1, 8)
    bx.barrier()
    if upto == "ln1":
        finish_early()
        return

    ar.alloc("xp", 16 * L)
    ar.alloc("wpan", 3 * 1024)
    ar.alloc("zt0", L); ar.alloc("zt1", L)
    XP = ar.bf("xp")
    WPAN = ar.bf("wpan")
    ZT = [ar.bf("zt0"), ar.bf("zt1")]
    WIN_R = T["a_WinT"].rearrange("(kt p) m -> p kt m", p=128)
    pe_mark = [0] * 32
    ev_mark = [0] * 32
    g_mark = [0] * 32
    yz_mark = [0] * 32
    for m in range(32):
        slot = m % 3
        gs = bx.GS[slot]
        if m >= 3:
            g.wait(PE, pe_mark[m - 3])
        elif m == 0:
            bx.g_sync()
        g.dma(out=WPAN[:, slot * 1024:(slot + 1) * 1024].rearrange("p (kt c) -> p kt c", kt=8),
              in_=WIN_R[:, :, m * 128:(m + 1) * 128]).then_inc(gs, 16)
        g_mark[m] = gs.n
        t.wait(gs, g_mark[m])
        if m >= 2:
            t.wait(S, ev_mark[m - 2])
        pset = psa if (m % 2 == 0) else psb
        ins = None
        for k in range(8):
            for c in range(4):
                ins = t.mm(pset(c), WPAN[:, slot * 1024 + k * 128: slot * 1024 + (k + 1) * 128],
                           XN1[:, k * L + c * 512: k * L + (c + 1) * 512],
                           start=(k == 0), stop=(k == 7))
        ins.then_inc(PE, 1)
        pe_mark[m] = PE.n
        s.wait(PE, pe_mark[m])
        if m < 16:
            ins = None
            for c in range(4):
                if True:
                    ins = s.act(XP[:, m * L + c * 512: m * L + (c + 1) * 512], pset(c), AF.Silu)
            ins.then_inc(S, 1)
            ev_mark[m] = S.n
        else:
            zslot = m % 2
            ys = bx.YS[zslot]
            if m >= 18:
                s.wait(ys, yz_mark[m - 2])
            ins = None
            for c in range(4):
                ins = s.act(ZT[zslot][:, c * 512:(c + 1) * 512], pset(c), AF.Silu)
            ins.then_inc(S, 1)
            ev_mark[m] = S.n
            y.wait(S, ev_mark[m])
            y.dma(out=T["sp_zs"][(m - 16) * 128:(m - 15) * 128, :], in_=ZT[zslot]).then_inc(ys, 16)
            yz_mark[m] = ys.n
    bx.barrier()
    ar.free("wpan", "zt0", "zt1", "xn1")
    if upto == "win":
        finish_early()
        return

    ar.alloc("wxp", 4 * 16 * 8)
    WXP = ar.bf("wxp")
    g.wait(PE, pe_mark[31])
    for qi, q in enumerate(("Be", "Bo", "Ce", "Co")):
        g.dma(out=WXP[:, qi * 128:(qi + 1) * 128].rearrange("p (kt c) -> p kt c", kt=16),
              in_=T["a_Wx" + q].rearrange("(kt p) m -> p kt m", p=128)).then_inc(G, 16)
    ar.alloc("wdtp", 2 * 16)
    WDTP = ar.bf("wdtp")
    g.dma(out=WDTP[:, 0:16], in_=T["a_wdt_col"]).then_inc(G, 16)
    g.dma(out=WDTP[:, 16:32], in_=T["a_wlam_col"]).then_inc(G, 16)
    ar.alloc("p2rows", 4 * L * 2)
    RO2 = ar.f32("p2rows")

    def row2(i):
        return RO2[0:1, i * L:(i + 1) * L]
    R_WDT, R_DTIN, R_LAM, R_CUM = range(4)
    y.dma(out=row2(R_WDT), in_=T["a_Wdt_row"]).then_inc(Y, 16)
    mark_wx_g = G.n
    mark_wdt_y = Y.n

    t.wait(G, mark_wx_g)
    ar.alloc("BpE", L * 2); ar.alloc("BpO", L * 2)
    ar.alloc("t64a", L * 2); ar.alloc("t64b", L * 2)
    BpE, BpO = ar.f32("BpE"), ar.f32("BpO")
    T64A, T64B = ar.f32("t64a"), ar.f32("t64b")
    bc_evac_mark = {}
    s_rstd_mark = 0
    for pair_i, (qe, qo) in enumerate((("Be", "Bo"), ("Ce", "Co"))):
        if pair_i == 1:
            t.wait(V, bc_evac_mark["Bo"])
            t.wait(S, s_rstd_mark)
        ins = None
        for k in range(16):
            for c in range(4):
                t.mm(PSAa[0:8, c * 512:(c + 1) * 512],
                     WXP[:, (0 if pair_i == 0 else 2) * 128 + k * 8: (0 if pair_i == 0 else 2) * 128 + (k + 1) * 8],
                     XP[:, k * L + c * 512: k * L + (c + 1) * 512], start=(k == 0), stop=(k == 15))
                ins = t.mm(PSBa[0:8, c * 512:(c + 1) * 512],
                           WXP[:, (1 if pair_i == 0 else 3) * 128 + k * 8: (1 if pair_i == 0 else 3) * 128 + (k + 1) * 8],
                           XP[:, k * L + c * 512: k * L + (c + 1) * 512], start=(k == 0), stop=(k == 15))
        ins.then_inc(PE, 1)
        v.wait(PE, PE.n)
        v.ts(BpE[0:8, 0:L], PSAa[0:8, 0:L], C["bias" + qe][:], None, op0=ALU.add)
        v.ts(BpO[0:8, 0:L], PSBa[0:8, 0:L], C["bias" + qo][:], None, op0=ALU.add).then_inc(V, 1)
        bc_evac_mark[qo] = V.n
        s.wait(V, V.n)
        s.act(T64A[0:8, 0:L], BpE[0:8, 0:L], AF.Square)
        s.act(T64B[0:8, 0:L], BpO[0:8, 0:L], AF.Square).then_inc(S, 1)
        t.wait(S, S.n)
        ins = None
        for c in range(4):
            t.mm(PSAa[0:1, c * 512:(c + 1) * 512], C["ones_f"][0:8, 0:1],
                 T64A[0:8, c * 512:(c + 1) * 512], start=True, stop=False)
            ins = t.mm(PSAa[0:1, c * 512:(c + 1) * 512], C["ones_f"][0:8, 0:1],
                       T64B[0:8, c * 512:(c + 1) * 512], start=False, stop=True)
        ins.then_inc(PE, 1)
        s.wait(PE, PE.n)
        s.act(row2(R_CUM), PSAa[0:1, 0:L], AF.Ln, scale=1.0 / 16, bias=RMS_EPS_A)
        s.act(row2(R_CUM), row2(R_CUM), AF.Exp, scale=-0.5).then_inc(S, 1)
        s_rstd_mark = S.n
        y.wait(S, S.n)
        y.dma(out=T["sp_rows"][4:5, :], in_=row2(R_CUM)).then_inc(Y, 16)
        y.wait(Y, Y.n)
        y.dma(out=T64A[:, 0:L], in_=T["sp_rows"][4:5, :].partition_broadcast(128)).then_inc(Y, 16)
        v.wait(Y, Y.n)
        v.stt(BpE[0:8, 0:L], BpE[0:8, 0:L], C["norm" + qe][:], T64A[0:8, 0:L], op0=ALU.mult, op1=ALU.mult)
        v.stt(BpO[0:8, 0:L], BpO[0:8, 0:L], C["norm" + qo][:], T64A[0:8, 0:L], op0=ALU.mult, op1=ALU.mult).then_inc(V, 1)
        bc_evac_mark[qe + "n"] = V.n
        if pair_i == 0:
            ar.alloc("BnBe", L * 2); ar.alloc("BnBo", L * 2)
            v.copy(ar.f32("BnBe")[0:8, 0:L], BpE[0:8, 0:L])
            v.copy(ar.f32("BnBo")[0:8, 0:L], BpO[0:8, 0:L]).then_inc(V, 1)
    mark_bc_done = V.n

    y.wait(V, mark_bc_done)
    y.dma(out=T["dbg1"][0:8, :], in_=BpE[0:8, 0:L]).then_inc(Y, 16)
    y.dma(out=T["dbg1"][8:16, :], in_=BpO[0:8, 0:L]).then_inc(Y, 16)
    y.dma(out=T["dbg1"][16:24, :], in_=ar.f32("BnBe")[0:8, 0:L]).then_inc(Y, 16)
    y.dma(out=T["dbg1"][24:32, :], in_=ar.f32("BnBo")[0:8, 0:L]).then_inc(Y, 16)
    t.wait(V, mark_bc_done)
    ins = None
    for k in range(16):
        for c in range(4):
            t.mm(PSAa[0:1, c * 512:(c + 1) * 512], WDTP[:, k:k + 1],
                 XP[:, k * L + c * 512: k * L + (c + 1) * 512], start=(k == 0), stop=(k == 15))
            ins = t.mm(PSBa[0:1, c * 512:(c + 1) * 512], WDTP[:, 16 + k:16 + k + 1],
                       XP[:, k * L + c * 512: k * L + (c + 1) * 512], start=(k == 0), stop=(k == 15))
    ins.then_inc(PE, 1)
    v.wait(PE, PE.n)
    v.copy(row2(R_DTIN), PSAa[0:1, 0:L])
    v.copy(row2(R_LAM), PSBa[0:1, 0:L]).then_inc(V, 1)
    s.wait(V, V.n)
    s.act(row2(R_LAM), row2(R_LAM), AF.Exp, scale=-1.0).then_inc(S, 1)
    v.wait(S, S.n)
    v.ts(row2(R_LAM), row2(R_LAM), 1.0, None, op0=ALU.add)
    v.op("drain")
    v.op("reciprocal", out=row2(R_LAM), in_=row2(R_LAM))
    v.op("drain").then_inc(V, 1)
    ar.alloc("lamb", L * 2)
    y.wait(V, V.n)
    y.wait(Y, mark_wdt_y)
    y.dma(out=T["sp_rows"][5:6, :], in_=row2(R_LAM)).then_inc(Y, 16)
    y.wait(Y, Y.n)
    y.dma(out=ar.f32("lamb"), in_=T["sp_rows"][5:6, :].partition_broadcast(128)).then_inc(Y, 16)
    mark_lamb = Y.n
    bx.barrier()
    if upto == "ssm":
        finish_early()
        return

    ar.free("wxp", "wdtp")
    ar.alloc("dtslot0", L); ar.alloc("dtslot1", L)
    ar.alloc("c2t", L * 2); ar.alloc("c1st", L * 2)
    ar.alloc("kxslot", L)
    DTS = [ar.bf("dtslot0"), ar.bf("dtslot1")]
    C2T, C1ST = ar.f32("c2t"), ar.f32("c1st")
    KXS = ar.bf("kxslot")
    va_mark = [0] * 16
    vdt_mark = [0] * 16
    ykx_mark = [0] * 16
    ydt_mark = [0] * 16
    vchain_mark = [0] * 16
    pcum_mark = [0] * 16
    v.wait(Y, mark_lamb)
    LAMB = ar.f32("lamb")
    for et in range(16):
        if et >= 1:
            t.wait(S, va_mark[et - 1])
        ins = None
        for c in range(4):
            ins = t.mm(psa(c), row2(R_WDT)[0:1, et * 128:(et + 1) * 128],
                       row2(R_DTIN)[0:1, c * 512:(c + 1) * 512], start=True, stop=True)
        ins.then_inc(PE, 1)
        s.wait(PE, PE.n)
        if et >= 1:
            s.wait(V, vchain_mark[et - 1])
        s.act(T64A[:, 0:L], PSAa[:, 0:L], AF.Relu, bias=C["bdt"][:, et:et + 1])
        s.act(T64B[:, 0:L], PSAa[:, 0:L], AF.Abs, bias=C["bdt"][:, et:et + 1]).then_inc(S, 1)
        va_mark[et] = S.n
        s.act(T64B[:, 0:L], T64B[:, 0:L], AF.Exp, scale=-1.0)
        s.act(T64B[:, 0:L], T64B[:, 0:L], AF.Ln, bias=1.0).then_inc(S, 1)
        v.wait(S, S.n)
        dts = DTS[et % 2]
        if et >= 2:
            v.wait(bx.YS1, ydt_mark[et - 2])
            v.wait(PE, pcum_mark[et - 2])
        v.tt(dts, T64A[:, 0:L], T64B[:, 0:L], op=ALU.add).then_inc(V, 1)
        vdt_mark[et] = V.n
        y.wait(V, vdt_mark[et])
        if et >= 2:
            pass
        y.dma(out=T["sp_dt"][et * 128:(et + 1) * 128, :], in_=dts).then_inc(bx.YS1, 16)
        ydt_mark[et] = bx.YS1.n
        t.wait(V, vdt_mark[et])
        ins = None
        for c in range(4):
            ins = t.mm(PSBa[0:1, c * 512:(c + 1) * 512], C["ones_b"][:],
                       dts[:, c * 512:(c + 1) * 512], start=(et == 0), stop=(et == 15))
        ins.then_inc(PE, 1)
        pcum_mark[et] = PE.n
        if et == 15:
            mark_cum_pe = PE.n
        v.tt(C2T, dts, LAMB, op=ALU.mult)
        v.tt(C1ST[:, 0:L - 1], dts[:, 1:L], C2T[:, 1:L], op=ALU.subtract)
        v.tt(T64A[:, 0:1], dts[:, 0:1], C1ST[:, 0:1], op=ALU.add)
        v.tt(C1ST[:, 0:L - 1], C2T[:, 0:L - 1], C1ST[:, 0:L - 1], op=ALU.add)
        v.copy(C1ST[:, L - 1:L], C2T[:, L - 1:L])
        v.copy(C1ST[:, 0:1], T64A[:, 0:1])
        v.op("drain")
        if et >= 1:
            v.wait(bx.YS0, ykx_mark[et - 1])
        v.tt(KXS, C1ST, XP[:, et * L:(et + 1) * L], op=ALU.mult).then_inc(V, 1)
        vchain_mark[et] = V.n
        y.wait(V, V.n)
        y.dma(out=T["sp_kxp"][et * 128:(et + 1) * 128, :], in_=KXS).then_inc(bx.YS0, 16)
        ykx_mark[et] = bx.YS0.n

    if upto == "dtpipe":
        finish_early()
        return
    ar.free("dtslot0", "dtslot1", "kxslot", "c2t")
    v.wait(PE, mark_cum_pe)
    v.copy(row2(R_CUM), PSBa[0:1, 0:L])
    v.memset(C1ST[0:1, 0:L], 1.0)
    v.op("drain")
    v.ts(row2(R_CUM), row2(R_CUM), 1.0 / E, None, op0=ALU.mult)
    v.op("drain")
    v.scan(row2(R_CUM), C1ST[0:1, 0:L], row2(R_CUM))
    v.op("drain").then_inc(V, 1)
    MIN = C["mini"]
    s.wait(V, V.n)
    s.act(MIN[0:1, 8:16], MIN[0:1, 0:8], AF.Abs)
    s.act(MIN[0:1, 16:24], MIN[0:1, 0:8], AF.Relu)
    s.act(MIN[0:1, 8:16], MIN[0:1, 8:16], AF.Exp, scale=-1.0)
    s.act(MIN[0:1, 8:16], MIN[0:1, 8:16], AF.Ln, bias=1.0).then_inc(S, 1)
    v.wait(S, S.n)
    v.tt(MIN[0:1, 24:32], MIN[0:1, 16:24], MIN[0:1, 8:16], op=ALU.add)
    v.op("drain").then_inc(V, 1)
    y.wait(V, V.n)
    y.dma(out=T["sp_rows"][7:8, :], in_=row2(R_CUM)).then_inc(Y, 16)
    y.dma(out=T["sp_rows"][6:7, 0:8], in_=MIN[0:1, 24:32]).then_inc(Y, 16)
    y.wait(Y, Y.n)
    y.dma(out=T64B[0:8, 0:L], in_=T["sp_rows"][7:8, :].partition_broadcast(8)).then_inc(Y, 16)
    y.dma(out=C["fcol"][:], in_=T["sp_rows"][6:7, 0:8].rearrange("a b -> b a")).then_inc(Y, 16)
    mark_ang = Y.n
    ar.alloc("sinA", L); ar.alloc("cosA", L); ar.alloc("trigtmp", L * 2)
    SINA, COSA = ar.bf("sinA"), ar.bf("cosA")
    TRT = ar.f32("trigtmp")
    v.wait(Y, mark_ang)
    v.ts(T64B[0:8, 0:L], T64B[0:8, 0:L], C["fcol"][:], None, op0=ALU.mult)
    v.op("drain").then_inc(V, 1)
    for which, dst, phase in (("sin", SINA, 0.0), ("cos", COSA, 0.25)):
        v.wait(S, S.n)
        v.ts(T64A[0:8, 0:L], T64B[0:8, 0:L], INV_2PI, phase, op0=ALU.mult, op1=ALU.add)
        v.op("drain")
        v.ts(TRT[0:8, 0:L], T64A[0:8, 0:L], RND_MAGIC, None, op0=ALU.add)
        v.op("drain")
        v.ts(TRT[0:8, 0:L], TRT[0:8, 0:L], -RND_MAGIC, None, op0=ALU.add)
        v.op("drain")
        v.tt(T64A[0:8, 0:L], T64A[0:8, 0:L], TRT[0:8, 0:L], op=ALU.subtract)
        v.op("drain")
        v.ts(T64A[0:8, 0:L], T64A[0:8, 0:L], TWO_PI, None, op0=ALU.mult)
        v.op("drain").then_inc(V, 1)
        s.wait(V, V.n)
        s.act(dst[0:8, 0:L], T64A[0:8, 0:L], AF.Sin).then_inc(S, 1)
    mark_trig = S.n

    if upto == "trig":
        finish_early()
        return
    ar.free("trigtmp")
    ar.alloc("BrEb", L); ar.alloc("BrOb", L); ar.alloc("CrEb", L); ar.alloc("CrOb", L)
    BRB = {"Be": ar.bf("BrEb"), "Bo": ar.bf("BrOb"), "Ce": ar.bf("CrEb"), "Co": ar.bf("CrOb")}
    v.wait(S, mark_trig)
    BnBe, BnBo = ar.f32("BnBe"), ar.f32("BnBo")
    for (ne, no, oute, outo) in ((BnBe, BnBo, BRB["Be"], BRB["Bo"]),
                                 (BpE, BpO, BRB["Ce"], BRB["Co"])):
        v.tt(T64A[0:8, 0:L], ne[0:8, 0:L], COSA[0:8, 0:L], op=ALU.mult)
        v.tt(T64B[0:8, 0:L], no[0:8, 0:L], SINA[0:8, 0:L], op=ALU.mult)
        v.op("drain")
        v.tt(oute[0:8, 0:L], T64A[0:8, 0:L], T64B[0:8, 0:L], op=ALU.subtract)
        v.tt(T64A[0:8, 0:L], ne[0:8, 0:L], SINA[0:8, 0:L], op=ALU.mult)
        v.tt(T64B[0:8, 0:L], no[0:8, 0:L], COSA[0:8, 0:L], op=ALU.mult)
        v.op("drain")
        v.tt(outo[0:8, 0:L], T64A[0:8, 0:L], T64B[0:8, 0:L], op=ALU.add)
    v.op("drain").then_inc(V, 1)
    mark_rope = V.n
    v.tt(T64A[0:8, 0:L], BRB["Be"][0:8, 0:L], BRB["Ce"][0:8, 0:L], op=ALU.mult)
    v.tt(T64B[0:8, 0:L], BRB["Bo"][0:8, 0:L], BRB["Co"][0:8, 0:L], op=ALU.mult)
    v.op("drain").then_inc(V, 1)
    t.wait(V, V.n)
    ins = None
    for c in range(4):
        t.mm(PSBa[0:1, c * 512:(c + 1) * 512], C["ones_f"][0:8, 0:1],
             T64A[0:8, c * 512:(c + 1) * 512], start=True, stop=False)
        ins = t.mm(PSBa[0:1, c * 512:(c + 1) * 512], C["ones_f"][0:8, 0:1],
                   T64B[0:8, c * 512:(c + 1) * 512], start=False, stop=True)
    ins.then_inc(PE, 1)
    v.wait(PE, PE.n)
    v.copy(row2(R_DTIN), PSBa[0:1, 0:L]).then_inc(V, 1)
    R_S = R_DTIN
    ar.free("sinA", "cosA", "BnBe", "BnBo", "BpE", "BpO")
    y.wait(V, mark_rope)
    y.dma(out=T["sp_br"][0:8, :], in_=BRB["Be"][0:8, 0:L]).then_inc(Y, 16)
    y.dma(out=T["sp_br"][8:16, :], in_=BRB["Bo"][0:8, 0:L]).then_inc(Y, 16)
    y.dma(out=T["sp_br"][16:24, :], in_=BRB["Ce"][0:8, 0:L]).then_inc(Y, 16)
    y.dma(out=T["sp_br"][24:32, :], in_=BRB["Co"][0:8, 0:L]).then_inc(Y, 16)
    y.wait(V, V.n)
    y.dma(out=T["sp_rows"][6:7, :], in_=row2(R_S)).then_inc(Y, 16)
    y.wait(Y, Y.n)
    y.dma(out=T64A[:, 0:L], in_=T["sp_rows"][6:7, :].partition_broadcast(128)).then_inc(Y, 16)
    mark_sb = Y.n
    v.ts(LAMB, LAMB, -1.0, 1.0, op0=ALU.mult, op1=ALU.add).then_inc(V, 1)
    mark_omlb = V.n
    bx.barrier()
    if upto == "rope":
        finish_early()
        return

    ar.alloc("dtslot0", L); ar.alloc("dtslot1", L); ar.alloc("kxslot", L)
    ar.alloc("c2t", L * 2)
    C2T = ar.f32("c2t")
    DTS = [ar.bf("dtslot0"), ar.bf("dtslot1")]
    KXS = ar.bf("kxslot")
    OMLB = LAMB
    SBB = T64A
    gdt_mark = [0] * 16
    gdt_used = [0] * 16
    yt2_mark = [0] * 16
    g.wait(bx.YS1, bx.YS1.n)
    g.wait(bx.YS0, bx.YS0.n)
    bx.g_sync()
    for et in range(16):
        gs = bx.GS[et % 2]
        if et >= 2:
            g.wait(V, gdt_used[et - 2])
        g.dma(out=DTS[et % 2], in_=T["sp_dt"][et * 128:(et + 1) * 128, :]).then_inc(gs, 16)
        gdt_mark[et] = gs.n
        v.wait(gs, gdt_mark[et])
        v.tt(C1ST[:, 0:L - 1], DTS[et % 2][:, 1:L], OMLB[:, 1:L], op=ALU.mult)
        v.tt(C1ST[:, 0:L - 1], C1ST[:, 0:L - 1], SBB[:, 0:L - 1], op=ALU.mult)
        v.memset(C1ST[:, L - 1:L], 0.0)
        v.op("drain")
        v.ts(C2T, C1ST, C["Da"][:, et:et + 1], -1.0, op0=ALU.subtract, op1=ALU.mult)
        if et >= 1:
            v.wait(bx.YS0, yt2_mark[et - 1])
        v.tt(KXS, C2T, XP[:, et * L:(et + 1) * L], op=ALU.mult).then_inc(V, 1)
        gdt_used[et] = V.n
        y.wait(V, V.n)
        y.dma(out=T["sp_T2"][et * 128:(et + 1) * 128, :], in_=KXS).then_inc(bx.YS0, 16)
        yt2_mark[et] = bx.YS0.n
    bx.barrier()
    if upto == "t2":
        finish_early()
        return

    ar.free("xp", "p2rows", "t64a", "t64b",
            "lamb", "dtslot0", "dtslot1", "c2t", "c1st", "kxslot",
            "BrEb", "BrOb", "CrEb", "CrOb")
    JB = 1026
    ar.alloc("arn1", 2 * 16 * JB)
    ar.alloc("arn2", 16 * JB)
    ar.alloc("brt", 16 * 1024); ar.alloc("crt", 16 * 1024)
    for nm in ("dts3", "kxs3", "t2s3", "zss3"):
        ar.alloc(nm, 2 * 1024)
    ar.alloc("carrysave", 16 * 16)
    ar.alloc("ytmp", 2 * 1024)
    ar.alloc("yzs", 2 * 1024)
    A1 = ar.f32("arn1").rearrange("p (j c) -> p j c", j=16)
    A2 = ar.bf("arn2").rearrange("p (j c) -> p j c", j=16)
    A1f = ar.f32("arn1")
    A2f = ar.bf("arn2")
    GC = ar.bf("arn1")
    BRT = ar.bf("brt")
    CRT = ar.bf("crt")
    SL = {nm: ar.bf(nm) for nm in ("dts3", "kxs3", "t2s3", "zss3")}
    CSAVE = ar.bf("carrysave").rearrange("p (e c) -> p e c", e=16)
    YTMP = ar.f32("ytmp")
    YZS = ar.bf("yzs")

    def brsrc(j, ch):
        r = (j // 2) if j % 2 == 0 else (8 + j // 2)
        return T["sp_br"][r:r + 1, ch * 1024:(ch + 1) * 1024]

    def crsrc(j, ch):
        r = (16 + j // 2) if j % 2 == 0 else (24 + j // 2)
        return T["sp_br"][r:r + 1, ch * 1024:(ch + 1) * 1024]

    sda_mark = [[0] * 16 for _ in range(2)]
    vbb_mark = [[0] * 16 for _ in range(2)]
    vgc_mark = [[0] * 16 for _ in range(2)]
    vev_mark = [[0] * 16 for _ in range(2)]
    pya_mark = [[0] * 16 for _ in range(2)]
    yyz_mark = [[0] * 16 for _ in range(2)]
    gld_mark = [[0] * 16 for _ in range(2)]

    def prev2(ch, et):
        if et >= 2:
            return (ch, et - 2)
        if ch == 1:
            return (0, 14 + et)
        return None

    g.wait(bx.YS0, bx.YS0.n)
    g.wait(bx.YS1, bx.YS1.n)
    g.wait(Y, Y.n)
    bx.g_sync()
    for ch in range(2):
        if ch == 1:
            y.wait(V, vbb_mark[0][15])
            y.wait(V, vgc_mark[0][15])
        for j in range(16):
            y.dma(out=BRT[:, j * 1024:(j + 1) * 1024], in_=brsrc(j, ch).partition_broadcast(128)).then_inc(Y, 16)
            y.dma(out=CRT[:, j * 1024:(j + 1) * 1024], in_=crsrc(j, ch).partition_broadcast(128)).then_inc(Y, 16)
        ybr_mark = Y.n
        for et in range(16):
            sl = et % 2
            p2 = prev2(ch, et)
            gs = bx.GS[sl]
            if p2 is not None:
                g.wait(S, sda_mark[p2[0]][p2[1]])
                g.wait(V, vev_mark[p2[0]][p2[1]])
            rows = slice(et * 128, (et + 1) * 128)
            cols = slice(ch * 1024, (ch + 1) * 1024)
            g.dma(out=SL["dts3"][:, sl * 1024:(sl + 1) * 1024], in_=T["sp_dt"][rows, cols]).then_inc(gs, 16)
            g.dma(out=SL["kxs3"][:, sl * 1024:(sl + 1) * 1024], in_=T["sp_kxp"][rows, cols]).then_inc(gs, 16)
            g.dma(out=SL["t2s3"][:, sl * 1024:(sl + 1) * 1024], in_=T["sp_T2"][rows, cols]).then_inc(gs, 16)
            g.dma(out=SL["zss3"][:, sl * 1024:(sl + 1) * 1024], in_=T["sp_zs"][rows, cols]).then_inc(gs, 16)
            gld_mark[ch][et] = gs.n
            pprev = (ch, et - 1) if et >= 1 else ((0, 15) if ch == 1 else None)
            if pprev is not None:
                v.wait(bx.PE, pya_mark[pprev[0]][pprev[1]])
            v.memset(A1[:, :, 0], 0.0)
            v.memset(A1[:, :, 1025], 0.0)
            v.memset(A2[:, :, 1025], 0.0)
            if ch == 0:
                v.memset(A2[:, :, 0], 0.0)
            else:
                v.copy(A2[:, :, 0], CSAVE[:, et, :])
            v.op("drain").then_inc(V, 1)
            vcarry_mark = V.n
            v.wait(bx.GS[sl], gld_mark[ch][et])
            v.wait(Y, ybr_mark)
            for j in range(16):
                v.tt(A2[:, j, 1:1025], SL["kxs3"][:, sl * 1024:(sl + 1) * 1024],
                     BRT[:, j * 1024:(j + 1) * 1024], op=ALU.mult)
            v.op("drain").then_inc(V, 1)
            vbb_mark[ch][et] = V.n
            s.wait(bx.GS[sl], gld_mark[ch][et])
            if pprev is not None:
                s.wait(bx.PE, pya_mark[pprev[0]][pprev[1]])
            s.wait(V, vcarry_mark)
            for j in range(16):
                s.act(A1[:, j, 1:1025], SL["dts3"][:, sl * 1024:(sl + 1) * 1024],
                      AF.Exp, scale=C["A"][:, et * 16 + j: et * 16 + j + 1])
            s.op("drain").then_inc(S, 1)
            sda_mark[ch][et] = S.n
            v.wait(S, sda_mark[ch][et])
            v.op("drain")
            v.scan(A2f[:, 0:16 * JB], A1f[:, 0:16 * JB], A2f[:, 0:16 * JB])
            v.op("drain").then_inc(V, 1)
            if ch == 0:
                v.copy(CSAVE[:, et, :], A2[:, :, 1024])
            for j in range(16):
                v.tt(GC[:, j * 1024:(j + 1) * 1024], A2[:, j, 1:1025],
                     CRT[:, j * 1024:(j + 1) * 1024], op=ALU.mult)
            v.op("drain").then_inc(V, 1)
            vgc_mark[ch][et] = V.n
            ps_y = PSAa if sl == 0 else PSBa
            t.wait(V, vgc_mark[ch][et])
            if p2 is not None:
                t.wait(V, vev_mark[p2[0]][p2[1]])
            ins = None
            for j in range(16):
                for c in range(2):
                    ins = t.mm(ps_y[:, c * 512:(c + 1) * 512], C["ident"][:],
                               GC[:, j * 1024 + c * 512: j * 1024 + (c + 1) * 512],
                               start=(j == 0), stop=(j == 15))
            ins.then_inc(PE, 1)
            pya_mark[ch][et] = PE.n
            v.wait(bx.PE, pya_mark[ch][et])
            if p2 is not None:
                v.wait(bx.YS[sl], yyz_mark[p2[0]][p2[1]])
            v.tt(YTMP[:, 0:1024], ps_y[:, 0:1024], SL["t2s3"][:, sl * 1024:(sl + 1) * 1024], op=ALU.add)
            v.tt(YZS[:, sl * 1024:(sl + 1) * 1024], YTMP[:, 0:1024],
                 SL["zss3"][:, sl * 1024:(sl + 1) * 1024], op=ALU.mult)
            v.op("drain").then_inc(V, 1)
            vev_mark[ch][et] = V.n
            y.wait(V, vev_mark[ch][et])
            y.dma(out=T["sp_yz"][rows, cols], in_=YZS[:, sl * 1024:(sl + 1) * 1024]).then_inc(bx.YS[sl], 16)
            yyz_mark[ch][et] = bx.YS[sl].n
    bx.barrier()
    if upto == "scan":
        finish_early()
        return

    ar.free("arn1", "arn2", "brt", "crt", "dts3", "kxs3", "t2s3", "zss3",
            "carrysave", "ytmp", "yzs")
    ar.alloc("wout", 16 * 1024)
    WOUT = ar.bf("wout").rearrange("p (kt m) -> p kt m", kt=16)
    WOUT_SRC = T["a_WoutT"].rearrange("(kt p) m -> p kt m", p=128)
    g.wait(bx.YS0, bx.YS0.n)
    g.wait(bx.YS1, bx.YS1.n)
    bx.g_sync()
    for kt in range(16):
        g.dma(out=WOUT[:, kt:kt + 1, :], in_=WOUT_SRC[:, kt:kt + 1, :]).then_inc(bx.GS2, 16)
    mark_wout_g = bx.GS2.n
    ar.alloc("yzr", 2 * 16 * 512)
    ar.alloc("x0s", 2 * 8 * 512 * 2)
    ar.alloc("x1s", 2 * 512 * 2)
    YZR = ar.bf("yzr")
    X0S = ar.f32("x0s")
    X1S = ar.f32("x1s")
    glc_mark = [0] * 4
    pvd_mark = [[0] * 8 for _ in range(4)]
    vx1_mark = [[0] * 8 for _ in range(4)]
    yx1_mark = [[0] * 8 for _ in range(4)]
    for lc in range(4):
        sl = lc % 2
        if lc >= 2:
            g.wait(bx.PE, pvd_mark[lc - 2][7])
            g.wait(V, vx1_mark[lc - 2][7])
        g.dma(out=YZR[:, sl * 8192: sl * 8192 + 8192].rearrange("p (kt c) -> p kt c", kt=16),
              in_=T["sp_yz"].rearrange("(kt p) l -> p kt l", p=128)[:, :, lc * 512:(lc + 1) * 512]).then_inc(bx.GS[sl], 16)
        g.dma(out=X0S[:, sl * 4096: sl * 4096 + 4096].rearrange("p (kt c) -> p kt c", kt=8),
              in_=T["xr0"].rearrange("(kt p) l -> p kt l", p=128)[:, :, lc * 512:(lc + 1) * 512]).then_inc(bx.GS[sl], 16)
        glc_mark[lc] = bx.GS[sl].n
        t.wait(bx.GS[sl], glc_mark[lc])
        if lc == 0:
            t.wait(bx.GS2, mark_wout_g)
        for d in range(8):
            psd = [psa(0), psa(1), psa(2), psb(0), psb(1), psb(2)][(lc * 8 + d) % 6]
            if lc * 8 + d >= 6:
                pd, dd = divmod(lc * 8 + d - 6, 8)
                t.wait(V, vx1_mark[pd][dd])
            ins = None
            for kt in range(16):
                ins = t.mm(psd, WOUT[:, kt, d * 128:(d + 1) * 128],
                           YZR[:, sl * 8192 + kt * 512: sl * 8192 + (kt + 1) * 512],
                           start=(kt == 0), stop=(kt == 15))
            ins.then_inc(bx.PE, 1)
            pvd_mark[lc][d] = bx.PE.n
            v.wait(bx.PE, pvd_mark[lc][d])
            ys = bx.YS[d % 2]
            if lc * 8 + d >= 2:
                pd, dd = divmod(lc * 8 + d - 2, 8)
                v.wait(bx.YS[d % 2], yx1_mark[pd][dd])
            x1t = X1S[:, (d % 2) * 512:(d % 2) * 512 + 512]
            v.tt(x1t, psd, X0S[:, sl * 4096 + d * 512: sl * 4096 + (d + 1) * 512], op=ALU.add).then_inc(V, 1)
            vx1_mark[lc][d] = V.n
            y.wait(V, vx1_mark[lc][d])
            y.dma(out=T["xr1"][d * 128:(d + 1) * 128, lc * 512:(lc + 1) * 512], in_=x1t).then_inc(ys, 16)
            if upto == "attn_a":
                y.dma(out=T["out"][d * 128:(d + 1) * 128, lc * 512:(lc + 1) * 512], in_=x1t).then_inc(ys, 16)
            yx1_mark[lc][d] = ys.n
    bx.barrier()
    ar.free("wout", "yzr", "x0s", "x1s")

    if upto not in ("attn_a",):
        emit_ffn(bx, T, ar, PSA, PSB, C, "xr1", "xr2", also_out=(upto == "ffn1"))
    if upto not in ("attn_a", "ffn1"):
        bupto = upto if upto != "full" else "b_full"
        emit_attn_b(bx, T, ar, PSA, PSB, C, upto=bupto)
    if upto == "full":
        emit_ffn(bx, T, ar, PSA, PSB, C, "xr3", "out", also_out=False)

    g.wait(bx.YS0, bx.YS0.n)
    g.wait(bx.YS1, bx.YS1.n)
    g.wait(Y, Y.n)
    for nm, srcnm in (("dt", "sp_dt"), ("kxp", "sp_kxp"), ("T2", "sp_T2"),
                      ("zs", "sp_zs"), ("yz", "sp_yz"), ("br", "sp_br"),
                      ("rows", "sp_rows")):
        if nm in TAPS:
            g.dma(out=TAPS[nm], in_=T[srcnm]).then_inc(G, 16)
    if "xr1" in TAPS:
        g.dma(out=TAPS["xr1"], in_=T["xr1"]).then_inc(G, 16)
    y.wait(G, G.n)
    y.wait(Y, Y.n)
    for sm in (bx.GS0, bx.GS1, bx.GS2, bx.YS0, bx.YS1, bx.GP):
        y.wait(sm, sm.n)
    bx.barrier()



def build_nc(taps=(), upto="attn_a"):
    nc = bass.Bass("TRN2", target_bir_lowering=False)
    T, TAPS = declare(nc, taps)
    import contextlib
    stack = contextlib.ExitStack()
    C = {}
    CONST_SPECS = [("ident", [128, 128], BF16), ("iot", [128, 128], F32),
                   ("ones_f", [128, 1], F32), ("ones_b", [128, 1], BF16),
                   ("cAlog", [128, 256], F32), ("A", [128, 256], F32),
                   ("bdt", [128, 16], F32), ("Da", [128, 16], F32)]
    for q in ("Be", "Bo", "Ce", "Co"):
        CONST_SPECS += [("bias" + q, [8, 1], F32), ("norm" + q, [8, 1], F32)]
    CONST_SPECS += [("mini", [1, 64], F32), ("cvals", [128, 2], F32), ("fcol", [8, 1], F32),
                    ("s1", [128, 32], F32), ("s2", [128, 32], F32),
                    ("s3", [128, 32], F32), ("s4", [128, 32], F32), ("s5", [128, 8], F32),
                    ("identF", [128, 128], F32), ("cw", [128, 68], F32), ("cb", [128, 17], F32),
                    ("bdtb", [32, 1], F32), ("bAcol", [32, 1], F32), ("Db", [128, 32], F32),
                    ("nwb", [128, 2048], F32), ("brmsw", [128, 8], F32),
                    ("onesrow", [1, 128], F32)]
    for nm, shape, dt in CONST_SPECS:
        C[nm] = stack.enter_context(nc.sbuf_tensor(nm, shape, dt)).ap()
    nc.const_aps.register(F32, LN_EPS, C["cvals"][:, 0:1])
    nc.const_aps.register(F32, RMS_EPS_A, C["cvals"][:, 1:2])
    ARN_EL = 99328
    arena_h = stack.enter_context(nc.sbuf_tensor("ARENA", [128, ARN_EL], BF16))
    PSA = stack.enter_context(nc.psum_tensor("PSA", [128, 512 * 4], F32))
    PSB = stack.enter_context(nc.psum_tensor("PSB", [128, 512 * 4], F32))
    sems = [Sem(stack.enter_context(nc.semaphore(n))) for n in
            ("sPE", "sV", "sS", "sY", "sG", "sYP", "sGP",
             "sGS0", "sGS1", "sGS2", "sYS0", "sYS1")]
    with nc.Block() as block:
        def body(real_engine, engine_name):
            for sm in sems:
                sm.n = 0
            ar = Arena(arena_h, ARN_EL)
            bx = B({engine_name: real_engine}, sems)
            emit_module(bx, T, TAPS, ar, PSA, PSB, C, upto=upto)

        block.tensor(lambda e: body(e, "tensor"))
        block.vector(lambda e: body(e, "vector"))
        block.scalar(lambda e: body(e, "scalar"))
        block.sync(lambda e: body(e, "sync"))
        block.gpsimd(lambda e: body(e, "gpsimd"))
    stack.close()
    return nc, T, TAPS


def run_cores(inputs, cores=(0,), taps=(), upto="attn_a"):
    nc, T, TAPS = build_nc(taps, upto)
    in_maps = []
    for b in cores:
        in_maps.append(prep_inputs(inputs, b))
    res = run_bass_kernel_spmd(nc, in_maps, core_ids=list(range(len(cores))))
    return res


def kernel(**inputs):
    nc, T, TAPS = build_nc(taps=(), upto="full")
    in_maps = [prep_inputs(inputs, b) for b in range(8)]
    res = run_bass_kernel_spmd(nc, in_maps, core_ids=list(range(8)))
    out = np.stack([np.asarray(res.results[b]["out"], dtype=np.float32).T
                    for b in range(8)], axis=0)
    return np.ascontiguousarray(out)
